# version 02
# baseline (speedup 1.0000x reference)
import numpy as np
import concourse.bass as bass
import concourse.mybir as mybir
from concourse.bass_utils import run_bass_kernel_spmd

F32 = mybir.dt.float32
AF = mybir.ActivationFunctionType
ALU = mybir.AluOpType
AX = mybir.AxisListType

D = 2048
H = 16
DK = 128
NIC = D // 128
TT = 512
NG = TT // 128
N_PRE = 12
N_OWN = 4
N_TILE = N_PRE + N_OWN
NE = 16
NJ = 6
EPS = 1e-5
SCALE = float(DK) ** -0.5
INV_D = 1.0 / D
INV_DK = 1.0 / DK
DEV_PHASES = {"attn", "attn_trk", "attn_trv", "attn_scan", "wo", "router", "experts", "epi"}
DEV_TAPS = set()


class Sched:
    ENG = ("pe", "dve", "act", "sync", "gp")

    def __init__(self):
        self.items = {e: [] for e in self.ENG}
        self.counts = {}
        self.async_evs = set()

    def add(self, eng, emit, waits=(), inc=None, amt=1, is_async=False):
        ev = None
        if inc is not None:
            self.counts[inc] = self.counts.get(inc, 0) + amt
            ev = (inc, self.counts[inc])
            if is_async:
                self.async_evs.add(ev)
        self.items[eng].append(
            (tuple(w for w in waits if w is not None), emit, (inc, amt) if inc else None)
        )
        return ev

    def now(self, sem):
        return (sem, self.counts.get(sem, 0))


class Ring:

    def __init__(self, S, queue, sem_base, bufs):
        self.S, self.q, self.bufs = S, queue, bufs
        self.n = len(bufs)
        self.sems = [f"{sem_base}{i}" for i in range(self.n)]
        self.loads = 0
        self.free_ev = [None] * self.n

    def load(self, dma_fn):
        slot = self.loads % self.n
        w = [self.free_ev[slot]] if self.free_ev[slot] is not None else []
        ev = self.S.add(
            self.q, (lambda h, b=self.bufs[slot], f=dma_fn: f(h, b)),
            waits=w, inc=self.sems[slot], amt=16,
        )
        self.loads += 1
        return slot, ev

    def release(self, slot, ev):
        self.free_ev[slot] = ev


def build_nc():
    nc = bass.Bass("TRN2", target_bir_lowering=False, debug=False)

    def dram_in(name, shape):
        return nc.dram_tensor(name, shape, F32, kind="ExternalInput").ap()

    d_x = dram_in("x", [N_TILE * TT, D])
    d_wq = dram_in("wq_t", [D, D])
    d_wk = dram_in("wk_t", [D, D])
    d_wv = dram_in("wv_t", [D, D])
    d_wo = dram_in("wo_t", [D, D])
    d_f = {n: dram_in(n, [128, 128]) for n in ("fq1_t", "fq2_t", "fk1_t", "fk2_t")}
    d_fb = {n: dram_in(n, [128, 1]) for n in ("fq1_b", "fq2_b", "fk1_b", "fk2_b")}
    d_nw = dram_in("norm_w", [128, 1])
    d_rw = dram_in("router_wt", [D, NE])
    d_eb = dram_in("e_bias", [1, NE])
    d_gu = dram_in("gu_t", [NE, D, 2 * NJ * 128])
    d_dw = dram_in("dw_t", [NE, NJ * 128, D])
    d_sg = dram_in("sg_t", [D, NJ * 128])
    d_su = dram_in("su_t", [D, NJ * 128])
    d_sd = dram_in("sd_t", [NJ * 128, D])
    d_out = nc.dram_tensor("out", [N_OWN * TT, D], F32, kind="ExternalOutput").ap()

    c_ident = nc.inline_tensor(np.eye(128, dtype=np.float32), name="c_ident")
    tril_np = (np.arange(128)[:, None] <= np.arange(128)[None, :]).astype(np.float32)
    c_tril = nc.inline_tensor(tril_np, name="c_tril")
    c_ones_col = nc.inline_tensor(np.ones((128, 1), np.float32), name="c_ones_col")
    c_ones_row = nc.inline_tensor(np.ones((1, 128), np.float32), name="c_ones_row")
    c_scale = nc.inline_tensor(np.full((128, 1), SCALE, np.float32), name="c_scale")

    def sb(name, shape, dt=F32):
        return nc.alloc_sbuf_tensor(name, shape, dt).ap()

    ident = sb("ident", [128, 128])
    tril = sb("tril", [128, 128])
    ones_col = sb("ones_col", [128, 1])
    ones_row = sb("ones_row", [1, 128])
    scale_col = sb("scale_col", [128, 1])
    fmw = {n: sb("s_" + n, [128, 128]) for n in d_f}
    fmb = {n: sb("s_" + n, [128, 1]) for n in d_fb}
    normw = sb("normw", [128, 1])
    routerw = sb("routerw", [128, NIC * NE])
    eb_row = sb("eb_row", [1, NE])
    ebbc = sb("ebbc", [128, NE])

    sb_x = sb("sb_x", [128, D])
    sb_xn = sb("sb_xn", [128, NIC * TT])
    sb_acc = sb("sb_acc", [128, NIC * TT])
    sb_orms = sb("sb_orms", [128, H * TT])
    sb_S = sb("sb_S", [128, H * 128])

    sb_pq = sb("sb_pq", [128, TT])
    sb_fa = sb("sb_fa", [128, TT])
    sb_fb2 = sb("sb_fb2", [128, TT])
    sb_pv = sb_fb2
    sb_qf = sb("sb_qf", [128, TT])
    sb_kf = sb("sb_kf", [128, TT])
    sb_ktok = sb("sb_ktok", [128, TT])
    sb_vtok = sb("sb_vtok", [128, TT])
    sb_sm = sb("sb_sm", [128, TT])
    sb_o = sb_fa
    sb_osq = sb_sm
    sb_stat = sb("sb_stat", [1, TT])

    sb_g = sb("sb_g", [128, TT])
    sb_u = sb("sb_u", [128, TT])
    sb_h = sb("sb_h", [128, NJ * TT])

    sb_z = sb("sb_z", [NE, TT])
    sb_sc = sb("sb_sc", [128, NG * NE])
    sb_bia = sb("sb_bia", [128, NE])
    sb_top = sb("sb_top", [128, 8])
    sb_msk = sb("sb_msk", [128, NE])
    sb_w4 = sb("sb_w4", [128, NE])
    sb_s4 = sb("sb_s4", [128, 1])
    sb_comb = sb("sb_comb", [NE, TT])

    sb_out = sb("sb_out", [128, D])
    sb_m = sb("sb_m", [128, 1])
    sb_var = sb("sb_var", [128, 1])
    sb_rstd = sb("sb_rstd", [128, 1])
    sb_msq = sb("sb_msq", [128, 1])
    sb_sq = sb("sb_sq", [128, 1])
    sb_m2 = sb("sb_m2", [128, 1])

    ringA_bufs = [sb(f"rA{i}", [128, NIC * 128]) for i in range(2)]
    ringB_bufs = [sb(f"rB{i}", [128, NJ * 128]) for i in range(2)]

    pA, pB, pC, pD, pE, pF, pG, pH = (
        nc.alloc_psum_tensor(f"p{i}", [128, TT], F32).ap() for i in range(8))
    pfree = {id(p): None for p in (pA, pB, pC, pD, pE, pF, pG, pH)}
    pread = {id(p): None for p in (pA, pB, pC, pD, pE, pF, pG, pH)}

    def rd(p, ev):
        pread[id(p)] = ev
        return ev

    S = Sched()
    on = S.add

    ringA = Ring(S, "sync", "sA", ringA_bufs)
    ringB = Ring(S, "gp", "sB", ringB_bufs)

    def dma(out_ap, in_ap):
        return lambda h: h.dma_start(out=out_ap, in_=in_ap)

    tap_prev = [None]

    def tap(name, ap, ev):
        if name not in DEV_TAPS:
            return None
        dt_ = nc.dram_tensor("tap_" + name, list(ap.shape), F32,
                             kind="ExternalOutput").ap()
        tev = on("gp", dma(dt_, ap), waits=[ev, tap_prev[0]], inc="sTap", amt=16)
        tap_prev[0] = tev
        return tev

    def slab16(dram2d, col0):
        return dram2d[:, col0:col0 + 128].rearrange("(ic p) m -> p ic m", p=128)

    def mm(h, out, lhsT, rhs, start, stop):
        return h.matmul(out, lhsT, rhs, start=start, stop=stop)

    def pe_acc(p_ap, pairs, extra_waits=()):
        n = len(pairs)
        ev = None
        for i, (l, r, w) in enumerate(pairs):
            ww = list(w) + (list(extra_waits) + [pread.get(id(p_ap))] if i == 0 else [])
            ev = on("pe",
                    (lambda h, o=p_ap, l_=l, r_=r, s=(i == 0), t=(i == n - 1):
                     mm(h, o[:], l_, r_, s, t)),
                    waits=ww,
                    inc="pe" if i == n - 1 else None)
        return ev

    boot_ld = None
    for out_ap, in_ap in [
        (ident[:], c_ident.ap()), (tril[:], c_tril.ap()),
        (ones_col[:], c_ones_col.ap()), (ones_row[:], c_ones_row.ap()),
        (scale_col[:], c_scale.ap()),
        (fmw["fq1_t"][:], d_f["fq1_t"]), (fmw["fq2_t"][:], d_f["fq2_t"]),
        (fmw["fk1_t"][:], d_f["fk1_t"]), (fmw["fk2_t"][:], d_f["fk2_t"]),
        (fmb["fq1_b"][:], d_fb["fq1_b"]), (fmb["fq2_b"][:], d_fb["fq2_b"]),
        (fmb["fk1_b"][:], d_fb["fk1_b"]), (fmb["fk2_b"][:], d_fb["fk2_b"]),
        (normw[:], d_nw),
        (routerw.rearrange("p (ic m) -> p ic m", ic=NIC),
         d_rw.rearrange("(ic p) m -> p ic m", p=128)),
        (eb_row[:], d_eb),
    ]:
        boot_ld = on("sync", dma(out_ap, in_ap), inc="sBoot", amt=16)

    ev_S0 = on("gp", lambda h: h.memset(sb_S[:], 0.0), inc="gp")
    ev = on("pe", lambda h: mm(h, pA[:, 0:NE], ones_row[:], eb_row[:], True, True),
            waits=[boot_ld], inc="pe")
    ev_ebbc = on("dve", lambda h: h.tensor_copy(ebbc[:], pA[:, 0:NE]), waits=[ev], inc="dve")
    pfree[id(pA)] = rd(pA, ev_ebbc)

    last_S_user = {h_: ev_S0 for h_ in range(H)}
    S_up_ev = {h_: ev_S0 for h_ in range(H)}
    x_free_ev = None
    xn_free_ev = None
    sbout_free_ev = None
    orms_free_ev = None
    acc_free_ev = None

    def xn_rhs(ic):
        return sb_xn[:, ic * TT:(ic + 1) * TT]

    for t in range(N_TILE):
        own = t >= N_PRE

        ln_last = None
        prev_var_reader = None
        prev_xn = None
        for tg in range(NG):
            r0 = t * TT + tg * 128
            x_ev = on("gp", dma(sb_x[:], d_x[r0:r0 + 128, :]),
                      waits=[x_free_ev], inc="sX", amt=16)
            xg = sb_x[:]
            e_sum = on("dve", lambda h, xg=xg: h.tensor_reduce(sb_m[:], xg, AX.X, ALU.add),
                       waits=[x_ev, prev_xn], inc="dve", is_async=True)
            e_sq = on("act", lambda h, xg=xg: h.activation(
                sb_orms[:, 0:D], xg, AF.Square, accum_out=sb_sq[:]),
                waits=[x_ev, orms_free_ev, prev_var_reader], inc="act")
            e_m = on("dve", lambda h: h.tensor_scalar(sb_m[:], sb_m[:], INV_D, None, ALU.mult),
                     waits=[e_sum], inc="dve")
            e_va = on("dve", lambda h: h.tensor_scalar(sb_var[:], sb_sq[:], INV_D, None, ALU.mult),
                      waits=[e_sq, e_m], inc="dve")
            e_mq = on("dve", lambda h: h.tensor_mul(sb_msq[:], sb_m[:], sb_m[:]),
                      waits=[e_m], inc="dve")
            e_vb = on("dve", lambda h: h.tensor_sub(sb_var[:], sb_var[:], sb_msq[:]),
                      waits=[e_va, e_mq], inc="dve")
            e_vc = on("dve", lambda h: h.tensor_scalar(sb_var[:], sb_var[:], EPS, None, ALU.add),
                      waits=[e_vb], inc="dve")
            e_sd = on("act", lambda h: h.activation(sb_var[:], sb_var[:], AF.Sqrt),
                      waits=[e_vc], inc="act")
            e_r = on("dve", lambda h: h.reciprocal(sb_rstd[:], sb_var[:]),
                     waits=[e_sd], inc="dve")
            prev_var_reader = e_r
            e_xn = on("dve", lambda h, xg=xg: h.tensor_scalar(
                sb_out[:], xg, sb_m[:, 0:1], sb_rstd[:, 0:1], ALU.subtract, ALU.mult),
                waits=[e_r, sbout_free_ev], inc="dve")
            ln_last = e_xn
            prev_xn = e_xn
            x_free_ev = e_xn
            tr_last = None
            for ic in range(NIC):
                pT = pG if ic % 2 == 0 else pH
                sl = (ic // 2) % 4
                e_t = on("pe", lambda h, pT=pT, sl=sl, ic=ic: h.transpose(
                    pT[:, sl * 128:(sl + 1) * 128], sb_out[:, ic * 128:(ic + 1) * 128],
                    ident[:]),
                    waits=[e_xn, boot_ld, pfree[id(pT)], pread[id(pT)], xn_free_ev],
                    inc="pe")
                e_c = on("dve", lambda h, pT=pT, sl=sl, ic=ic, tg=tg: h.tensor_copy(
                    sb_xn[:, ic * TT + tg * 128: ic * TT + (tg + 1) * 128],
                    pT[:, sl * 128:(sl + 1) * 128]),
                    waits=[e_t], inc="dve")
                pfree[id(pT)] = rd(pT, e_c)
                tr_last = e_t
            sbout_free_ev = tr_last
        xn_ready = S.now("dve")
        if t == N_PRE:
            tap("xn", sb_xn[:], xn_ready)
            tap("lnm", sb_m[:], xn_ready)
            tap("lnr", sb_rstd[:], xn_ready)
            tap("lnsq", sb_sq[:], xn_ready)
            _tev = tap("xtok", sb_x[:], xn_ready)
            if _tev is not None:
                x_free_ev = _tev
        xn_free_ev = None

        sb_fa_free = None
        for hh in (range(H) if "attn" in DEV_PHASES else []):
            if own:
                sq_slot, sq_ev = ringA.load(lambda h, b, hh=hh: h.dma_start(
                    out=b.rearrange("p (ic m) -> p ic m", ic=NIC), in_=slab16(d_wq, hh * 128)))
                bufq = ringA_bufs[sq_slot]
            sk_slot, sk_ev = ringA.load(lambda h, b, hh=hh: h.dma_start(
                out=b.rearrange("p (ic m) -> p ic m", ic=NIC), in_=slab16(d_wk, hh * 128)))
            bufk = ringA_bufs[sk_slot]

            ev_qf = None
            ev_fb_q = None
            if own:
                ev_q = pe_acc(pA, [
                    (bufq[:, ic * 128:(ic + 1) * 128], xn_rhs(ic),
                     [sq_ev, xn_ready] if ic == 0 else [])
                    for ic in range(NIC)], extra_waits=[pfree[id(pA)]])
                ringA.release(sq_slot, ev_q)
                ev_qe = on("act", lambda h: h.copy(sb_pq[:], pA[:]), waits=[ev_q], inc="act")
                pfree[id(pA)] = rd(pA, ev_qe)
                ev_fa_q = on("pe", lambda h: mm(h, pC[:], fmw["fq1_t"][:], sb_pq[:], True, True),
                             waits=[ev_qe, pfree[id(pC)], pread[id(pC)]], inc="pe")
                ev_fb_q = on("pe", lambda h: mm(h, pD[:], fmw["fq2_t"][:], sb_pq[:], True, True),
                             waits=[pfree[id(pD)], pread[id(pD)]], inc="pe")
                e1 = on("dve", lambda h: h.tensor_scalar(
                    sb_fa[:], pC[:], fmb["fq1_b"][:, 0:1], scale_col[:, 0:1],
                    ALU.add, ALU.mult),
                    waits=[ev_fa_q, sb_fa_free], inc="dve")
                pfree[id(pC)] = rd(pC, e1)
                e2 = on("dve", lambda h: h.tensor_scalar(
                    sb_fb2[:], pD[:], fmb["fq2_b"][:, 0:1], None, ALU.add),
                    waits=[ev_fb_q], inc="dve")
                pfree[id(pD)] = rd(pD, e2)
                ev_qf = on("dve", lambda h: h.tensor_mul(sb_qf[:], sb_fa[:], sb_fb2[:]),
                           waits=[e1, e2], inc="dve")
                if t == N_PRE and hh == H - 1:
                    tap("qf", sb_qf[:], ev_qf)

            ev_k = pe_acc(pB, [
                (bufk[:, ic * 128:(ic + 1) * 128], xn_rhs(ic),
                 [sk_ev, xn_ready] if ic == 0 else [])
                for ic in range(NIC)], extra_waits=[pfree[id(pB)]])
            ringA.release(sk_slot, ev_k)
            ev_ke = on("act", lambda h: h.copy(sb_pq[:], pB[:]),
                       waits=[ev_k, ev_fb_q], inc="act")
            pfree[id(pB)] = rd(pB, ev_ke)
            ev_fa_k = on("pe", lambda h: mm(h, pC[:], fmw["fk1_t"][:], sb_pq[:], True, True),
                         waits=[ev_ke, pfree[id(pC)], pread[id(pC)]], inc="pe")
            ev_fb_k = on("pe", lambda h: mm(h, pD[:], fmw["fk2_t"][:], sb_pq[:], True, True),
                         waits=[pfree[id(pD)], pread[id(pD)]], inc="pe")
            e1 = on("dve", lambda h: h.tensor_scalar(
                sb_fa[:], pC[:], fmb["fk1_b"][:, 0:1], None, ALU.add),
                waits=[ev_fa_k, ev_qf, sb_fa_free], inc="dve")
            pfree[id(pC)] = rd(pC, e1)
            e2 = on("dve", lambda h: h.tensor_scalar(
                sb_fb2[:], pD[:], fmb["fk2_b"][:, 0:1], None, ALU.add),
                waits=[ev_fb_k, ev_qf], inc="dve")
            pfree[id(pD)] = rd(pD, e2)
            ev_kf = on("dve", lambda h: h.tensor_mul(sb_kf[:], sb_fa[:], sb_fb2[:]),
                       waits=[e1, e2], inc="dve")
            if not own:
                sb_fa_free = ev_kf
            if t == N_PRE and hh == H - 1:
                tap("kf", sb_kf[:], ev_kf)

            sv_slot, sv_ev = ringA.load(lambda h, b, hh=hh: h.dma_start(
                out=b.rearrange("p (ic m) -> p ic m", ic=NIC), in_=slab16(d_wv, hh * 128)))
            bufv = ringA_bufs[sv_slot]
            ev_v = pe_acc(pA, [
                (bufv[:, ic * 128:(ic + 1) * 128], xn_rhs(ic),
                 [sv_ev] if ic == 0 else [])
                for ic in range(NIC)], extra_waits=[pfree[id(pA)]])
            ringA.release(sv_slot, ev_v)
            ev_ve = on("act", lambda h: h.copy(sb_pv[:], pA[:]), waits=[ev_v, ev_kf], inc="act")
            pfree[id(pA)] = rd(pA, ev_ve)

            kt_evs, vt_evs = [], []
            for n in (range(NG) if "attn_trk" in DEV_PHASES else []):
                e_t = on("pe", lambda h, n=n: h.transpose(
                    pE[:, n * 128:(n + 1) * 128], sb_kf[:, n * 128:(n + 1) * 128], ident[:]),
                    waits=[ev_kf, pread[id(pE)]] + ([pfree[id(pE)]] if n == 0 else []),
                    inc="pe")
                e_c = on("dve", lambda h, n=n: h.tensor_copy(
                    sb_ktok[:, n * 128:(n + 1) * 128], pE[:, n * 128:(n + 1) * 128]),
                    waits=[e_t], inc="dve")
                rd(pE, e_c)
                kt_evs.append(e_c)
            if kt_evs:
                pfree[id(pE)] = kt_evs[-1]
            for n in (range(NG) if "attn_trv" in DEV_PHASES else []):
                e_t = on("pe", lambda h, n=n: h.transpose(
                    pF[:, n * 128:(n + 1) * 128], sb_pv[:, n * 128:(n + 1) * 128], ident[:]),
                    waits=[ev_ve, pread[id(pF)]] + ([pfree[id(pF)]] if n == 0 else []),
                    inc="pe")
                e_c = on("dve", lambda h, n=n: h.tensor_copy(
                    sb_vtok[:, n * 128:(n + 1) * 128], pF[:, n * 128:(n + 1) * 128]),
                    waits=[e_t], inc="dve")
                rd(pF, e_c)
                vt_evs.append(e_c)
            if vt_evs:
                pfree[id(pF)] = vt_evs[-1]
            if t == N_PRE and hh == H - 1 and vt_evs:
                tap("vtok", sb_vtok[:], vt_evs[-1])
                tap("ktok", sb_ktok[:], kt_evs[-1])

            for n in (range(NG) if "attn_scan" in DEV_PHASES else []):
                cs = slice(n * 128, (n + 1) * 128)
                if own:
                    e_sT = on("pe", lambda h, cs=cs: mm(
                        h, pG[:, cs], sb_kf[:, cs], sb_qf[:, cs], True, True),
                        waits=[ev_kf, ev_qf, pread[id(pG)]]
                              + ([pfree[id(pG)]] if n == 0 else []), inc="pe")
                    e_sm = on("dve", lambda h, cs=cs: h.tensor_mul(
                        sb_sm[:, cs], pG[:, cs], tril[:]), waits=[e_sT], inc="dve")
                    rd(pG, e_sm)
                    e_int = on("pe", lambda h, cs=cs, hh=hh: mm(
                        h, pH[:, cs], sb_S[:, hh * 128:(hh + 1) * 128], sb_qf[:, cs],
                        True, False),
                        waits=[S_up_ev[hh], ev_qf, pread[id(pH)]]
                              + ([pfree[id(pH)]] if n == 0 else []),
                        inc="pe")
                    last_S_user[hh] = e_int
                    e_o = on("pe", lambda h, cs=cs: mm(
                        h, pH[:, cs], sb_vtok[:, cs], sb_sm[:, cs], False, True),
                        waits=[e_sm, vt_evs[n]], inc="pe")
                e_kv = on("pe", lambda h, cs=cs: mm(
                    h, pE[:, cs], sb_ktok[:, cs], sb_vtok[:, cs], True, True),
                    waits=[kt_evs[n], vt_evs[n], kt_evs[-1], pread[id(pE)]], inc="pe")
                e_Sup = on("dve", lambda h, cs=cs, hh=hh: h.tensor_add(
                    sb_S[:, hh * 128:(hh + 1) * 128], sb_S[:, hh * 128:(hh + 1) * 128],
                    pE[:, cs]),
                    waits=[e_kv, last_S_user[hh]], inc="dve")
                rd(pE, e_Sup)
                S_up_ev[hh] = e_Sup
                if n == NG - 1:
                    pfree[id(pE)] = e_Sup

                if own:
                    e_oc = on("dve", lambda h, cs=cs: h.tensor_copy(sb_o[:, cs], pH[:, cs]),
                              waits=[e_o, ev_kf], inc="dve")
                    rd(pH, e_oc)
                    e_osq = on("act", lambda h, cs=cs: h.activation(
                        sb_osq[:, cs], pH[:, cs], AF.Square), waits=[e_o], inc="act")
                    rd(pH, e_osq)
                    e_ss = on("pe", lambda h, cs=cs: mm(
                        h, pG[0:1, cs], ones_col[:], sb_osq[:, cs], True, True),
                        waits=[e_osq, e_sm, pread[id(pG)]], inc="pe")
                    e_st = on("dve", lambda h, cs=cs: h.tensor_scalar(
                        sb_stat[:, cs], pG[0:1, cs], INV_DK, EPS, ALU.mult, ALU.add),
                        waits=[e_ss], inc="dve")
                    rd(pG, e_st)
                    e_sd = on("act", lambda h, cs=cs: h.activation(
                        sb_stat[:, cs], sb_stat[:, cs], AF.Sqrt), waits=[e_st], inc="act")
                    e_rc = on("dve", lambda h, cs=cs: h.reciprocal(
                        sb_stat[:, cs], sb_stat[:, cs]), waits=[e_sd], inc="dve")
                    e_bc = on("pe", lambda h, cs=cs: mm(
                        h, pF[:, cs], ones_row[:], sb_stat[0:1, cs], True, True),
                        waits=[e_rc, vt_evs[-1], pread[id(pF)]], inc="pe")
                    e_or = on("dve", lambda h, cs=cs, hh=hh, n=n: h.scalar_tensor_tensor(
                        sb_orms[:, hh * TT + n * 128: hh * TT + (n + 1) * 128],
                        sb_o[:, cs], normw[:, 0:1], pF[:, cs], ALU.mult, ALU.mult),
                        waits=[e_oc, e_bc,
                               orms_free_ev if (hh == 0 and n == 0) else None],
                        inc="dve")
                    rd(pF, e_or)
                    if n == NG - 1:
                        pfree[id(pF)] = e_or
                        pfree[id(pH)] = e_or
                        pfree[id(pG)] = e_or
                        sb_fa_free = e_or
                        if t == N_PRE and hh == H - 1:
                            tap("orms0", sb_orms[:, 0:TT], e_or)
                            tap("S0", sb_S[:, 0:128], S_up_ev[0])
                            tap("o_last", sb_o[:], e_or)
            if not own:
                if NG > 0:
                    pfree[id(pE)] = S_up_ev[hh]
                    pfree[id(pF)] = vt_evs[-1]

        last_orms_write = S.now("dve")

        acc_guard = None
        if own:
            last_wo_mm = None
            for oc in (range(NIC) if ("wo" in DEV_PHASES and "attn" in DEV_PHASES) else []):
                so_slot, so_ev = ringA.load(lambda h, b, oc=oc: h.dma_start(
                    out=b.rearrange("p (ic m) -> p ic m", ic=NIC), in_=slab16(d_wo, oc * 128)))
                bufo = ringA_bufs[so_slot]
                pW = pA if oc % 2 == 0 else pB
                ev_w = pe_acc(pW, [
                    (bufo[:, hh2 * 128:(hh2 + 1) * 128], sb_orms[:, hh2 * TT:(hh2 + 1) * TT],
                     [so_ev, last_orms_write] if hh2 == 0 else [])
                    for hh2 in range(H)], extra_waits=[pfree[id(pW)]])
                ringA.release(so_slot, ev_w)
                last_wo_mm = ev_w
                ev_e = on("dve", lambda h, oc=oc, pW=pW: h.tensor_copy(
                    sb_acc[:, oc * TT:(oc + 1) * TT], pW[:]),
                    waits=[ev_w, acc_free_ev if oc == 0 else None], inc="dve")
                pfree[id(pW)] = rd(pW, ev_e)
            orms_free_ev = last_wo_mm
            if t == N_PRE:
                acc_guard = tap("acc0", sb_acc[:, 0:TT], S.now("dve"))

            if "router" not in DEV_PHASES:
                ev_comb = None
            ev_r = None if "router" not in DEV_PHASES else pe_acc(pC[0:NE, :], [
                (routerw[:, ic * NE:(ic + 1) * NE], xn_rhs(ic), [])
                for ic in range(NIC)], extra_waits=[pfree[id(pC)]])
            ev_ze = on("act", lambda h: h.copy(sb_z[:], pC[0:NE, :]), waits=[ev_r], inc="act")
            pfree[id(pC)] = rd(pC, ev_ze)
            last_sg = None
            comb_t_last = None
            prev_cb = None
            for g in range(NG):
                gs = slice(g * 128, (g + 1) * 128)
                e_t = on("pe", lambda h, gs=gs, g=g: mm(
                    h, pE[:, g * 128: g * 128 + NE], sb_z[:, gs], ident[0:NE, 0:NE],
                    True, True),
                    waits=[ev_ze, pread[id(pE)]] + ([pfree[id(pE)]] if g == 0 else []),
                    inc="pe")
                e_sg = on("act", lambda h, g=g: h.activation(
                    sb_sc[:, g * NE:(g + 1) * NE], pE[:, g * 128: g * 128 + NE], AF.Sigmoid),
                    waits=[e_t], inc="act")
                rd(pE, e_sg)
                last_sg = e_sg
                e_b = on("dve", lambda h, g=g: h.tensor_add(
                    sb_bia[:], sb_sc[:, g * NE:(g + 1) * NE], ebbc[:]),
                    waits=[e_sg, ev_ebbc, prev_cb], inc="dve")
                e_m8 = on("dve", lambda h: h.max(out=sb_top[:], in_=sb_bia[:]),
                          waits=[e_b], inc="dve", is_async=True)
                e_mk = on("dve", lambda h: h.tensor_scalar(
                    sb_msk[:], sb_bia[:], sb_top[:, 3:4], None, ALU.is_ge),
                    waits=[e_m8], inc="dve")
                e_w4 = on("dve", lambda h, g=g: h.tensor_mul(
                    sb_w4[:], sb_msk[:], sb_sc[:, g * NE:(g + 1) * NE]),
                    waits=[e_mk], inc="dve")
                e_s4 = on("dve", lambda h: h.tensor_reduce(
                    sb_s4[:], sb_w4[:], AX.X, ALU.add),
                    waits=[e_w4], inc="dve", is_async=True)
                e_s4b = on("dve", lambda h: h.tensor_scalar(
                    sb_s4[:], sb_s4[:], 1e-20, None, ALU.add), waits=[e_s4], inc="dve")
                e_rc = on("dve", lambda h: h.reciprocal(sb_s4[:], sb_s4[:]),
                          waits=[e_s4b], inc="dve")
                e_cb = on("dve", lambda h: h.tensor_scalar(
                    sb_w4[:], sb_w4[:], sb_s4[:, 0:1], None, ALU.mult),
                    waits=[e_rc], inc="dve")
                prev_cb = e_cb
                e_ct = on("pe", lambda h, g=g: mm(
                    h, pC[0:NE, g * 128:(g + 1) * 128], sb_w4[:], ident[:], True, True),
                    waits=[e_cb, pread[id(pC)]] + ([pfree[id(pC)]] if g == 0 else []),
                    inc="pe")
                comb_t_last = e_ct
            pfree[id(pE)] = last_sg
            ev_comb = on("dve", lambda h: h.tensor_copy(sb_comb[:], pC[0:NE, :]),
                         waits=[comb_t_last], inc="dve")
            if t == N_PRE:
                tap("comb", sb_comb[:], ev_comb)
                tap("scores", sb_sc[:], ev_comb)
            pfree[id(pC)] = rd(pC, ev_comb)

            cbc_free = ev_comb
            last_acc_add = None
            last_h0 = None
            for e in (range(NE + 1) if "experts" in DEV_PHASES else []):
                routed = e < NE
                ev_cbc = None
                if routed:
                    e_rx = on("pe", lambda h, e=e: mm(
                        h, pD[0:1, :], ident[0:NE, e:e + 1], sb_comb[:], True, True),
                        waits=[ev_comb, pfree[id(pD)], pread[id(pD)], cbc_free], inc="pe")
                    e_rs = on("dve", lambda h: h.tensor_copy(sb_stat[0:1, :], pD[0:1, :]),
                              waits=[e_rx], inc="dve")
                    rd(pD, e_rs)
                    ev_cbc = on("pe", lambda h: mm(
                        h, pD[:], ones_row[:], sb_stat[0:1, :], True, True),
                        waits=[e_rs, pread[id(pD)]], inc="pe")
                h_evs = []
                for j in range(NJ):
                    if routed:
                        gslot, g_ev = ringA.load(lambda h, b, e=e, j=j: h.dma_start(
                            out=b.rearrange("p (ic m) -> p ic m", ic=NIC),
                            in_=slab16(d_gu[e], (2 * j) * 128)))
                        uslot, u_ev = ringA.load(lambda h, b, e=e, j=j: h.dma_start(
                            out=b.rearrange("p (ic m) -> p ic m", ic=NIC),
                            in_=slab16(d_gu[e], (2 * j + 1) * 128)))
                    else:
                        gslot, g_ev = ringA.load(lambda h, b, j=j: h.dma_start(
                            out=b.rearrange("p (ic m) -> p ic m", ic=NIC),
                            in_=slab16(d_sg, j * 128)))
                        uslot, u_ev = ringA.load(lambda h, b, j=j: h.dma_start(
                            out=b.rearrange("p (ic m) -> p ic m", ic=NIC),
                            in_=slab16(d_su, j * 128)))
                    bufg, bufu = ringA_bufs[gslot], ringA_bufs[uslot]
                    ev_g = pe_acc(pA, [
                        (bufg[:, ic * 128:(ic + 1) * 128], xn_rhs(ic),
                         [g_ev] if ic == 0 else [])
                        for ic in range(NIC)], extra_waits=[pfree[id(pA)]])
                    ringA.release(gslot, ev_g)
                    ev_u = pe_acc(pB, [
                        (bufu[:, ic * 128:(ic + 1) * 128], xn_rhs(ic),
                         [u_ev] if ic == 0 else [])
                        for ic in range(NIC)], extra_waits=[pfree[id(pB)]])
                    ringA.release(uslot, ev_u)
                    e_sil = on("act", lambda h: h.activation(sb_g[:], pA[:], AF.Silu),
                               waits=[ev_g, last_h0], inc="act")
                    pfree[id(pA)] = rd(pA, e_sil)
                    e_ue = on("act", lambda h: h.copy(sb_u[:], pB[:]),
                              waits=[ev_u, last_h0], inc="act")
                    pfree[id(pB)] = rd(pB, e_ue)
                    e_h0 = on("dve", lambda h, j=j: h.tensor_mul(
                        sb_h[:, j * TT:(j + 1) * TT], sb_g[:], sb_u[:]),
                        waits=[e_sil, e_ue], inc="dve")
                    last_h0 = e_h0
                    if routed:
                        e_h = on("dve", lambda h, j=j: h.tensor_mul(
                            sb_h[:, j * TT:(j + 1) * TT],
                            sb_h[:, j * TT:(j + 1) * TT], pD[:]),
                            waits=[e_h0, ev_cbc], inc="dve")
                        rd(pD, e_h)
                    else:
                        e_h = e_h0
                    h_evs.append(e_h)
                if routed:
                    cbc_free = h_evs[-1]
                    pfree[id(pD)] = h_evs[-1]
                for oc in range(NIC):
                    if routed:
                        dslot, d_ev = ringB.load(lambda h, b, e=e, oc=oc: h.dma_start(
                            out=b.rearrange("p (ic m) -> p ic m", ic=NJ),
                            in_=slab16(d_dw[e], oc * 128)))
                    else:
                        dslot, d_ev = ringB.load(lambda h, b, oc=oc: h.dma_start(
                            out=b.rearrange("p (ic m) -> p ic m", ic=NJ),
                            in_=slab16(d_sd, oc * 128)))
                    bufd = ringB_bufs[dslot]
                    pW = pE if oc % 2 == 0 else pF
                    ev_d = pe_acc(pW, [
                        (bufd[:, jc * 128:(jc + 1) * 128], sb_h[:, jc * TT:(jc + 1) * TT],
                         [d_ev, h_evs[jc]])
                        for jc in range(NJ)], extra_waits=[pfree[id(pW)]])
                    ringB.release(dslot, ev_d)
                    ev_a = on("dve", lambda h, oc=oc, pW=pW: h.tensor_add(
                        sb_acc[:, oc * TT:(oc + 1) * TT],
                        sb_acc[:, oc * TT:(oc + 1) * TT], pW[:]),
                        waits=[ev_d, acc_guard], inc="dve")
                    pfree[id(pW)] = rd(pW, ev_a)
                    last_acc_add = ev_a

            last_x_read = None
            last_acc_read = None
            for tg in range(NG):
                r0x = t * TT + tg * 128
                x_ev = on("gp", dma(sb_x[:], d_x[r0x:r0x + 128, :]),
                          waits=[x_free_ev], inc="sX", amt=16)
                st_evs = []
                for oc in range(NIC):
                    pT = pG if oc % 2 == 0 else pH
                    sl = (oc // 2) % 4
                    e_t = on("pe", lambda h, pT=pT, sl=sl, oc=oc, tg=tg: h.transpose(
                        pT[:, sl * 128:(sl + 1) * 128],
                        sb_acc[:, oc * TT + tg * 128: oc * TT + (tg + 1) * 128], ident[:]),
                        waits=[pfree[id(pT)], pread[id(pT)],
                               last_acc_add if (tg == 0 and oc == 0) else None],
                        inc="pe")
                    last_acc_read = e_t
                    e_a = on("dve", lambda h, pT=pT, sl=sl, oc=oc: h.tensor_add(
                        sb_out[:, oc * 128:(oc + 1) * 128],
                        pT[:, sl * 128:(sl + 1) * 128],
                        sb_x[:, oc * 128:(oc + 1) * 128]),
                        waits=[e_t, x_ev, sbout_free_ev if oc == 0 else None], inc="dve")
                    pfree[id(pT)] = rd(pT, e_a)
                    last_x_read = e_a
                    st_evs.append(e_a)
                x_free_ev = st_evs[-1]
                r0 = (t - N_PRE) * TT + tg * 128
                ev_st = on("gp", dma(d_out[r0:r0 + 128, :], sb_out[:]),
                           waits=[st_evs[-1]], inc="sO", amt=16)
                sbout_free_ev = ev_st
            acc_free_ev = last_acc_read
            xn_free_ev = S.now("pe")
        else:
            xn_free_ev = S.now("pe")

    total_sO = S.counts.get("sO", 0)
    on("gp", lambda h: h.nop(), waits=[("sO", total_sO)])

    sem_names = ["pe", "dve", "act", "gp", "sBoot", "sA0", "sA1",
                 "sB0", "sB1", "sX", "sO", "sTap"]
    import contextlib
    with contextlib.ExitStack() as stack:
        semmap = {n: stack.enter_context(nc.semaphore("sem_" + n)) for n in sem_names}
        block = stack.enter_context(nc.Block())
        own_sem = {"pe": "pe", "dve": "dve", "act": "act", "gp": "gp"}

        def make_runner(name):
            def run(h):
                waited = {}
                for waits, emit, inc in S.items[name]:
                    for (sem, val) in waits:
                        if val > waited.get(sem, 0):
                            h.wait_ge(semmap[sem], val)
                            waited[sem] = val
                    inst = emit(h)
                    if inc is not None:
                        inst.then_inc(semmap[inc[0]], inc[1])
            return run

        block.tensor(make_runner("pe"))
        block.vector(make_runner("dve"))
        block.scalar(make_runner("act"))
        block.sync(make_runner("sync"))
        block.gpsimd(make_runner("gp"))

    return nc


def _prep_inputs(inputs):
    f = lambda a: np.ascontiguousarray(np.asarray(a, dtype=np.float32))
    x = f(inputs["L0_attn_x"])
    gu = f(inputs["L0_ffn_gate_up_proj"])
    dw = f(inputs["L0_ffn_down_proj"])
    com = {
        "wq_t": f(np.asarray(inputs["L0_attn_wq"]).T),
        "wk_t": f(np.asarray(inputs["L0_attn_wk"]).T),
        "wv_t": f(np.asarray(inputs["L0_attn_wv"]).T),
        "wo_t": f(np.asarray(inputs["L0_attn_wo"]).T),
        "fq1_t": f(np.asarray(inputs["L0_attn_fq1_w"]).T),
        "fq2_t": f(np.asarray(inputs["L0_attn_fq2_w"]).T),
        "fk1_t": f(np.asarray(inputs["L0_attn_fk1_w"]).T),
        "fk2_t": f(np.asarray(inputs["L0_attn_fk2_w"]).T),
        "fq1_b": f(inputs["L0_attn_fq1_b"]).reshape(128, 1),
        "fq2_b": f(inputs["L0_attn_fq2_b"]).reshape(128, 1),
        "fk1_b": f(inputs["L0_attn_fk1_b"]).reshape(128, 1),
        "fk2_b": f(inputs["L0_attn_fk2_b"]).reshape(128, 1),
        "norm_w": f(inputs["L0_attn_norm_w"]).reshape(128, 1),
        "router_wt": f(np.asarray(inputs["L0_ffn_router_w"]).T),
        "e_bias": f(inputs["L0_ffn_e_bias"]).reshape(1, NE),
        "sg_t": f(np.asarray(inputs["L0_ffn_shared_gate_w"]).T),
        "su_t": f(np.asarray(inputs["L0_ffn_shared_up_w"]).T),
        "sd_t": f(np.asarray(inputs["L0_ffn_shared_down_w"]).T),
    }
    gut = np.transpose(gu, (0, 2, 1))
    idx = []
    for j in range(NJ):
        idx.extend(range(j * 128, (j + 1) * 128))
        idx.extend(range(768 + j * 128, 768 + (j + 1) * 128))
    com["gu_t"] = f(gut[:, :, idx])
    com["dw_t"] = f(np.transpose(dw, (0, 2, 1)))

    in_maps = []
    for c in range(8):
        b, q = c // 4, c % 4
        own = x[b, q * 2048:(q + 1) * 2048]
        pre = x[b, :q * 2048]
        pad = np.zeros((N_PRE * TT - q * 2048, D), np.float32)
        xc = np.concatenate([pad, pre, own], axis=0)
        m = dict(com)
        m["x"] = np.ascontiguousarray(xc)
        in_maps.append(m)
    return in_maps


def kernel(**inputs):
    in_maps = _prep_inputs(inputs)
    nc = build_nc()
    res = run_bass_kernel_spmd(nc, in_maps, core_ids=list(range(8)))
    out = np.empty((2, 8192, 2048), np.float32)
    for c in range(8):
        b, q = c // 4, c % 4
        out[b, q * 2048:(q + 1) * 2048] = res.results[c]["out"]
    return out


# version 06 (final)
# speedup vs baseline: 1.0029x; 1.0029x over previous
import numpy as np
import concourse.bass as bass
import concourse.mybir as mybir
from concourse.bass_utils import run_bass_kernel_spmd

F32 = mybir.dt.float32
BF16 = mybir.dt.bfloat16
AF = mybir.ActivationFunctionType
ALU = mybir.AluOpType
AX = mybir.AxisListType

D = 2048
H = 16
DK = 128
NIC = D // 128
TT = 512
NG = TT // 128
N_PRE = 12
N_OWN = 4
N_TILE = N_PRE + N_OWN
NE = 16
NJ = 6
EPS = 1e-5
SCALE = float(DK) ** -0.5
INV_D = 1.0 / D
INV_DK = 1.0 / DK
DEV_PHASES = {"attn", "attn_trk", "attn_trv", "attn_scan", "wo", "router", "experts", "epi"}
DEV_TAPS = set()
DEV_SKIP = set()


class Sched:
    ENG = ("pe", "dve", "act", "sync", "gp")

    def __init__(self):
        self.items = {e: [] for e in self.ENG}
        self.counts = {}

    def add(self, eng, emit, waits=(), inc=None, amt=1):
        ev = None
        if inc is not None:
            self.counts[inc] = self.counts.get(inc, 0) + amt
            ev = (inc, self.counts[inc])
        self.items[eng].append(
            (tuple(w for w in waits if w is not None), emit, (inc, amt) if inc else None)
        )
        return ev

    def now(self, sem):
        return (sem, self.counts.get(sem, 0))


class Ring:

    def __init__(self, S, queue, sem_base, bufs):
        self.S, self.q, self.bufs = S, queue, bufs
        self.n = len(bufs)
        self.sems = [f"{sem_base}{i}" for i in range(self.n)]
        self.loads = 0
        self.free_ev = [None] * self.n

    def load(self, dma_fn, extra_waits=()):
        slot = self.loads % self.n
        w = [self.free_ev[slot]] + list(extra_waits)
        ev = self.S.add(
            self.q, (lambda h, sl=slot, f=dma_fn: f(h, sl)),
            waits=w, inc=self.sems[slot], amt=16,
        )
        self.loads += 1
        return slot, ev

    def release(self, slot, ev):
        self.free_ev[slot] = ev


def build_nc():
    nc = bass.Bass("TRN2", target_bir_lowering=False, debug=False)

    def dram_in(name, shape):
        return nc.dram_tensor(name, shape, F32, kind="ExternalInput").ap()

    d_x = dram_in("x", [N_TILE * TT, D])
    d_wq = dram_in("wq_t", [D, D])
    d_wk = dram_in("wk_t", [D, D])
    d_wv = dram_in("wv_t", [D, D])
    d_wo = dram_in("wo_t", [D, D])
    d_f = {n: dram_in(n, [128, 128]) for n in ("fq1_t", "fq2_t", "fk1_t", "fk2_t")}
    d_fb = {n: dram_in(n, [128, 1]) for n in ("fq1_b", "fq2_b", "fk1_b", "fk2_b")}
    d_nw = dram_in("norm_w", [128, 1])
    d_rw = dram_in("router_wt", [D, NE])
    d_eb = dram_in("e_bias", [1, NE])
    d_gu = dram_in("gu_t", [NE, D, 2 * NJ * 128])
    d_dw = dram_in("dw_t", [NE, NJ * 128, D])
    d_sg = dram_in("sg_t", [D, NJ * 128])
    d_su = dram_in("su_t", [D, NJ * 128])
    d_sd = dram_in("sd_t", [NJ * 128, D])
    d_out = nc.dram_tensor("out", [N_OWN * TT, D], F32, kind="ExternalOutput").ap()

    def scr(name, nslab, rows):
        return nc.dram_tensor(name, [nslab, 2 * rows, 128], BF16, kind="Internal").ap()

    s_wo = scr("wo_s", NIC, D)
    s_gu = scr("gu_s", NE * 2 * NJ, D)
    s_sg = scr("sg_s", NJ, D)
    s_su = scr("su_s", NJ, D)

    c_ident = nc.inline_tensor(np.eye(128, dtype=np.float32), name="c_ident")
    tril_np = (np.arange(128)[:, None] <= np.arange(128)[None, :]).astype(np.float32)
    c_tril = nc.inline_tensor(tril_np, name="c_tril")
    c_ones_col = nc.inline_tensor(np.ones((128, 1), np.float32), name="c_ones_col")
    c_ones_row = nc.inline_tensor(np.ones((1, 128), np.float32), name="c_ones_row")
    c_scale = nc.inline_tensor(np.full((128, 1), SCALE, np.float32), name="c_scale")

    def sb(name, shape, dt=F32):
        return nc.alloc_sbuf_tensor(name, shape, dt).ap()

    ident = sb("ident", [128, 128])
    tril = sb("tril", [128, 128])
    ones_col = sb("ones_col", [128, 1])
    ones_row = sb("ones_row", [1, 128])
    scale_col = sb("scale_col", [128, 1])
    fmw = {n: sb("s_" + n, [128, 128]) for n in d_f}
    fmb = {n: sb("s_" + n, [128, 1]) for n in d_fb}
    normw = sb("normw", [128, 1])
    routerw = sb("routerw", [128, NIC * NE])
    eb_row = sb("eb_row", [1, NE])
    ebbc = sb("ebbc", [128, NE])

    sb_x = sb("sb_x", [128, D])
    sb_xnh = sb("sb_xnh", [128, NIC * TT], BF16)
    sb_xnl = sb("sb_xnl", [128, NIC * TT], BF16)
    sb_acc = sb("sb_acc", [128, NIC * TT])
    sb_oh = sb("sb_oh", [128, H * TT], BF16)
    _ol_off = (nc.sbuf_base + 31) & ~31
    sb_ol = sb("sb_ol", [128, H * TT], BF16)
    sb_olf = nc.alloc_sbuf_tensor_at("sb_olf", [128, H * TT // 2], F32,
                                     offset=_ol_off).ap()
    sb_S = sb("sb_S", [128, H * 128])

    sb_pq = sb("sb_pq", [128, TT])
    sb_fa = sb("sb_fa", [128, TT])
    sb_fb2 = sb("sb_fb2", [128, TT])
    sb_pv = sb_fb2
    sb_qf = sb("sb_qf", [128, TT])
    sb_kf = sb("sb_kf", [128, TT])
    sb_ktok = sb("sb_ktok", [128, TT])
    sb_vtok = sb("sb_vtok", [128, TT])
    sb_sm = sb("sb_sm", [128, TT])
    sb_o = sb_fa
    sb_osq = sb_sm
    sb_stat = sb("sb_stat", [1, TT])

    sb_g = sb("sb_g", [128, TT])
    sb_u = sb("sb_u", [128, TT])
    sb_h = sb("sb_h", [128, NJ * TT])

    sb_z = sb("sb_z", [NE, TT])
    sb_sc = sb("sb_sc", [128, NG * NE])
    sb_bia = sb("sb_bia", [128, NE])
    sb_top = sb("sb_top", [128, 8])
    sb_msk = sb("sb_msk", [128, NE])
    sb_w4 = sb("sb_w4", [128, NE])
    sb_s4 = sb("sb_s4", [128, 1])
    sb_comb = sb("sb_comb", [NE, TT])

    sb_out = sb("sb_out", [128, D])
    sb_m = sb("sb_m", [128, 1])
    sb_var = sb("sb_var", [128, 1])
    sb_rstd = sb("sb_rstd", [128, 1])
    sb_msq = sb("sb_msq", [128, 1])
    sb_sq = sb("sb_sq", [128, 1])

    ringA_f, ringA_b = [], []
    for i in range(2):
        _off = (nc.sbuf_base + 31) & ~31
        ringA_f.append(sb(f"rA{i}f", [128, NIC * 128]))
        ringA_b.append(nc.alloc_sbuf_tensor_at(
            f"rA{i}b", [128, 2 * NIC * 128], BF16, offset=_off).ap())
    ringB_bufs = [sb(f"rB{i}", [128, NJ * 128]) for i in range(2)]

    pA, pB, pC, pD, pE, pF, pG, pH = (
        nc.alloc_psum_tensor(f"p{i}", [128, TT], F32).ap() for i in range(8))
    pfree = {id(p): None for p in (pA, pB, pC, pD, pE, pF, pG, pH)}
    pread = {id(p): None for p in (pA, pB, pC, pD, pE, pF, pG, pH)}

    def rd(p, ev):
        pread[id(p)] = ev
        return ev

    S = Sched()
    on = S.add

    ringA = Ring(S, "sync", "sA", ringA_f)
    ringB = Ring(S, "gp", "sB", ringB_bufs)

    def dma(out_ap, in_ap):
        return lambda h: h.dma_start(out=out_ap, in_=in_ap)

    tap_prev = [None]

    def tap(name, ap, ev):
        if name not in DEV_TAPS:
            return None
        dt_ = nc.dram_tensor("tap_" + name, list(ap.shape), F32,
                             kind="ExternalOutput").ap()
        tev = on("gp", dma(dt_, ap), waits=[ev, tap_prev[0]], inc="sTap", amt=16)
        tap_prev[0] = tev
        return tev

    def slab16(dram2d, col0):
        return dram2d[:, col0:col0 + 128].rearrange("(ic p) m -> p ic m", p=128)

    def mm(h, out, lhsT, rhs, start, stop):
        return h.matmul(out, lhsT, rhs, start=start, stop=stop)

    def pe_acc(p_ap, pairs, extra_waits=()):
        n = len(pairs)
        ev = None
        for i, (l, r, w) in enumerate(pairs):
            ww = list(w) + (list(extra_waits) + [pread.get(id(p_ap))] if i == 0 else [])
            ev = on("pe",
                    (lambda h, o=p_ap, l_=l, r_=r, s=(i == 0), t=(i == n - 1):
                     mm(h, o[:], l_, r_, s, t)),
                    waits=ww,
                    inc="pe" if i == n - 1 else None)
        return ev

    def pe_acc3(p_ap, buf, nch, rh_fn, rl_fn, ic_waits, extra_waits=(), out_sl=None):
        o = p_ap[:] if out_sl is None else p_ap[:, out_sl]
        total = nch * 3
        idx = 0
        ev = None
        for ic in range(nch):
            lh = buf[:, ic * 128:(ic + 1) * 128]
            ll = buf[:, (nch + ic) * 128:(nch + ic + 1) * 128]
            for l_, r_ in ((lh, rh_fn(ic)), (lh, rl_fn(ic)), (ll, rh_fn(ic))):
                ww = list(ic_waits(ic)) if idx % 3 == 0 else []
                if idx == 0:
                    ww += list(extra_waits) + [pread.get(id(p_ap))]
                ev = on("pe",
                        (lambda h, o=o, l2=l_, r2=r_, s=(idx == 0), t=(idx == total - 1):
                         mm(h, o, l2, r2, s, t)),
                        waits=ww,
                        inc="pe" if idx == total - 1 else None)
                idx += 1
        return ev

    boot_ld = None
    for out_ap, in_ap in [
        (ident[:], c_ident.ap()), (tril[:], c_tril.ap()),
        (ones_col[:], c_ones_col.ap()), (ones_row[:], c_ones_row.ap()),
        (scale_col[:], c_scale.ap()),
        (fmw["fq1_t"][:], d_f["fq1_t"]), (fmw["fq2_t"][:], d_f["fq2_t"]),
        (fmw["fk1_t"][:], d_f["fk1_t"]), (fmw["fk2_t"][:], d_f["fk2_t"]),
        (fmb["fq1_b"][:], d_fb["fq1_b"]), (fmb["fq2_b"][:], d_fb["fq2_b"]),
        (fmb["fk1_b"][:], d_fb["fk1_b"]), (fmb["fk2_b"][:], d_fb["fk2_b"]),
        (normw[:], d_nw),
        (routerw.rearrange("p (ic m) -> p ic m", ic=NIC),
         d_rw.rearrange("(ic p) m -> p ic m", p=128)),
        (eb_row[:], d_eb),
    ]:
        boot_ld = on("sync", dma(out_ap, in_ap), inc="sBoot", amt=16)

    ev_S0 = on("gp", lambda h: h.memset(sb_S[:], 0.0), inc="gp")
    ev = on("pe", lambda h: mm(h, pA[:, 0:NE], ones_row[:], eb_row[:], True, True),
            waits=[boot_ld], inc="pe")
    ev_ebbc = on("dve", lambda h: h.tensor_copy(ebbc[:], pA[:, 0:NE]), waits=[ev], inc="dve")
    pfree[id(pA)] = rd(pA, ev_ebbc)

    pp_items = []
    if "prepass" not in DEV_SKIP:
        for i in range(NIC):
            pp_items.append(("all", d_wo, i * 128, NIC, s_wo, i))
        for i in range(NJ):
            pp_items.append(("all", d_sg, i * 128, NIC, s_sg, i))
        for i in range(NJ):
            pp_items.append(("all", d_su, i * 128, NIC, s_su, i))
        for e in range(NE):
            for j2 in range(2 * NJ):
                pp_items.append(("all", d_gu[e], j2 * 128, NIC, s_gu, e * 2 * NJ + j2))

    pp_free = [[] for _ in range(2)]
    ready_all = {}
    for i, (grp, src, col0, nch, dst, sidx) in enumerate(pp_items):
        slot = i % 2
        w = nch * 128
        stF = sb_olf[:, slot * 2048: slot * 2048 + w]
        stH = sb_oh[:, slot * 4096: slot * 4096 + w]
        stL = sb_oh[:, slot * 4096 + 2048: slot * 4096 + 2048 + w]
        e_in = on("sync", (lambda h, src=src, col0=col0, nch=nch, stF=stF:
                          h.dma_start(
                              out=stF.rearrange("p (ic m) -> p ic m", ic=nch),
                              in_=src[:, col0:col0 + 128].rearrange(
                                  "(ic p) m -> p ic m", p=128))),
                  waits=pp_free[slot], inc=f"sPin{slot}", amt=16)
        e_hi = on("act", (lambda h, stH=stH, stF=stF: h.copy(stH, stF)),
                  waits=[e_in], inc="act")
        e_lo = on("dve", (lambda h, stL=stL, stF=stF, stH=stH:
                          h.tensor_sub(stL, stF, stH)),
                  waits=[e_in, e_hi], inc="dve")
        e_oh = on("gp", (lambda h, dst=dst, sidx=sidx, nch=nch, stH=stH:
                          h.dma_start(
                              out=dst[sidx, 0:nch * 128, :].rearrange(
                                  "(ic p) m -> p ic m", p=128),
                              in_=stH.rearrange("p (ic m) -> p ic m", ic=nch))),
                  waits=[e_hi], inc=f"sPoh{slot}", amt=16)
        e_ol = on("gp", (lambda h, dst=dst, sidx=sidx, nch=nch, stL=stL:
                          h.dma_start(
                              out=dst[sidx, nch * 128:2 * nch * 128, :].rearrange(
                                  "(ic p) m -> p ic m", p=128),
                              in_=stL.rearrange("p (ic m) -> p ic m", ic=nch))),
                  waits=[e_lo], inc=f"sPol{slot}", amt=16)
        pp_free[slot] = [e_oh, e_ol]
        ready_all[f"h{slot}"] = e_oh
        ready_all[f"l{slot}"] = e_ol
    ready_all = list(ready_all.values())

    last_S_user = {h_: ev_S0 for h_ in range(H)}
    S_up_ev = {h_: ev_S0 for h_ in range(H)}
    x_free_ev = None
    xn_free_ev = None
    sbout_free_ev = None
    orms_free_ev = None
    acc_free_ev = None
    acc_free_ev2 = None
    oh_guard = ready_all
    h_free_ev = None
    sb_fa_free = None
    sb_fa_free_act = None

    def xnh_rhs(ic):
        return sb_xnh[:, ic * TT:(ic + 1) * TT]

    def xnl_rhs(ic):
        return sb_xnl[:, ic * TT:(ic + 1) * TT]

    def xn32_rhs(ic):
        return sb_acc[:, ic * TT:(ic + 1) * TT]

    def slab_load(ring, dst3, sidx, nch, ready):
        return ring.load(
            (lambda h, sl, d=dst3, s=sidx, nch=nch: h.dma_start(
                out=ringA_b[sl].rearrange("p (hl ic m) -> p hl ic m", hl=2, ic=nch),
                in_=d[s].rearrange("(hl ic p) m -> p hl ic m", hl=2, p=128))),
            extra_waits=ready)

    for t in range(N_TILE):
        own = t >= N_PRE

        ln_last = None
        prev_var_reader = None
        prev_xn = None
        for tg in range(NG):
            r0 = t * TT + tg * 128
            x_ev = on("gp", dma(sb_x[:], d_x[r0:r0 + 128, :]),
                      waits=[x_free_ev], inc="sX", amt=16)
            xg = sb_x[:]
            e_sum = on("dve", lambda h, xg=xg: h.tensor_reduce(sb_m[:], xg, AX.X, ALU.add),
                       waits=[x_ev, prev_xn], inc="dve")
            e_sq = on("act", lambda h, xg=xg: h.activation(
                sb_h[:, 0:D], xg, AF.Square, accum_out=sb_sq[:]),
                waits=[x_ev, h_free_ev, prev_var_reader], inc="act")
            e_m = on("dve", lambda h: h.tensor_scalar(sb_m[:], sb_m[:], INV_D, None, ALU.mult),
                     waits=[e_sum], inc="dve")
            e_va = on("dve", lambda h: h.tensor_scalar(sb_var[:], sb_sq[:], INV_D, None, ALU.mult),
                      waits=[e_sq, e_m], inc="dve")
            e_mq = on("dve", lambda h: h.tensor_mul(sb_msq[:], sb_m[:], sb_m[:]),
                      waits=[e_m], inc="dve")
            e_vb = on("dve", lambda h: h.tensor_sub(sb_var[:], sb_var[:], sb_msq[:]),
                      waits=[e_va, e_mq], inc="dve")
            e_vc = on("dve", lambda h: h.tensor_scalar(sb_var[:], sb_var[:], EPS, None, ALU.add),
                      waits=[e_vb], inc="dve")
            e_sd = on("act", lambda h: h.activation(sb_var[:], sb_var[:], AF.Sqrt),
                      waits=[e_vc], inc="act")
            e_r = on("dve", lambda h: h.reciprocal(sb_rstd[:], sb_var[:]),
                     waits=[e_sd], inc="dve")
            prev_var_reader = e_r
            e_xn = on("dve", lambda h, xg=xg: h.tensor_scalar(
                sb_out[:], xg, sb_m[:, 0:1], sb_rstd[:, 0:1], ALU.subtract, ALU.mult),
                waits=[e_r, sbout_free_ev], inc="dve")
            ln_last = e_xn
            prev_xn = e_xn
            x_free_ev = e_xn
            tr_last = None
            for ic in range(NIC):
                pT = pG if ic % 2 == 0 else pH
                sl = (ic // 2) % 4
                csl = slice(ic * TT + tg * 128, ic * TT + (tg + 1) * 128)
                e_t = on("pe", lambda h, pT=pT, sl=sl, ic=ic: h.transpose(
                    pT[:, sl * 128:(sl + 1) * 128], sb_out[:, ic * 128:(ic + 1) * 128],
                    ident[:]),
                    waits=[e_xn, boot_ld, pfree[id(pT)], pread[id(pT)], xn_free_ev],
                    inc="pe")
                if "c32" not in DEV_SKIP:
                    e_c32 = on("dve", lambda h, pT=pT, sl=sl, csl=csl: h.tensor_copy(
                        sb_acc[:, csl], pT[:, sl * 128:(sl + 1) * 128]),
                        waits=[e_t,
                               acc_free_ev if (tg == 0 and ic == 0) else None,
                               acc_free_ev2 if (tg == 0 and ic == 0) else None],
                        inc="dve")
                    rd(pT, e_c32)
                    last_ev = e_c32
                else:
                    e_c32 = None
                    last_ev = e_t
                if own and "xnsplit" not in DEV_SKIP:
                    e_ch = on("act", lambda h, pT=pT, sl=sl, csl=csl: h.copy(
                        sb_xnh[:, csl], pT[:, sl * 128:(sl + 1) * 128]),
                        waits=[e_t, e_c32], inc="act")
                    rd(pT, e_ch)
                    e_cl = on("dve", lambda h, pT=pT, sl=sl, csl=csl: h.tensor_sub(
                        sb_xnl[:, csl], pT[:, sl * 128:(sl + 1) * 128], sb_xnh[:, csl]),
                        waits=[e_t, e_ch], inc="dve")
                    rd(pT, e_cl)
                    last_ev = e_cl
                pfree[id(pT)] = last_ev
                tr_last = e_t
            sbout_free_ev = tr_last
        xn_ready = S.now("dve")
        xn_ready_act = S.now("act")
        if t == N_PRE:
            tap("xnh", sb_xnh[:], xn_ready)
            tap("xnl", sb_xnl[:], xn_ready)
        xn_free_ev = None

        pq_guard = [None, None]
        last_oh = None
        last_ol = None
        for hh in (range(H) if "attn" in DEV_PHASES else []):
            if own:
                sq_slot, sq_ev = ringA.load(lambda h, sl, hh=hh: h.dma_start(
                    out=ringA_f[sl].rearrange("p (ic m) -> p ic m", ic=NIC),
                    in_=slab16(d_wq, hh * 128)))
                bufq = ringA_f[sq_slot]
            sk_slot, sk_ev = ringA.load(lambda h, sl, hh=hh: h.dma_start(
                out=ringA_f[sl].rearrange("p (ic m) -> p ic m", ic=NIC),
                in_=slab16(d_wk, hh * 128)))
            bufk = ringA_f[sk_slot]

            ev_qf = None
            ev_fb_q = None
            if own:
                ev_q = pe_acc(pA, [
                    (bufq[:, ic * 128:(ic + 1) * 128], xn32_rhs(ic),
                     [sq_ev, xn_ready] if ic == 0 else [])
                    for ic in range(NIC)], extra_waits=[pfree[id(pA)]])
                ringA.release(sq_slot, ev_q)
                ev_qe = on("act", lambda h: h.copy(sb_pq[:], pA[:]),
                           waits=[ev_q, pq_guard[0], pq_guard[1]], inc="act")
                pfree[id(pA)] = rd(pA, ev_qe)
                ev_fa_q = on("pe", lambda h: mm(h, pC[:], fmw["fq1_t"][:], sb_pq[:], True, True),
                             waits=[ev_qe, pfree[id(pC)], pread[id(pC)]], inc="pe")
                ev_fb_q = on("pe", lambda h: mm(h, pD[:], fmw["fq2_t"][:], sb_pq[:], True, True),
                             waits=[pfree[id(pD)], pread[id(pD)]], inc="pe")
                e1 = on("dve", lambda h: h.tensor_scalar(
                    sb_fa[:], pC[:], fmb["fq1_b"][:, 0:1], scale_col[:, 0:1],
                    ALU.add, ALU.mult),
                    waits=[ev_fa_q, sb_fa_free, sb_fa_free_act], inc="dve")
                pfree[id(pC)] = rd(pC, e1)
                e2 = on("dve", lambda h: h.tensor_scalar(
                    sb_fb2[:], pD[:], fmb["fq2_b"][:, 0:1], None, ALU.add),
                    waits=[ev_fb_q], inc="dve")
                pfree[id(pD)] = rd(pD, e2)
                ev_qf = on("dve", lambda h: h.tensor_mul(sb_qf[:], sb_fa[:], sb_fb2[:]),
                           waits=[e1, e2], inc="dve")
                if t == N_PRE and hh == H - 1:
                    tap("qf", sb_qf[:], ev_qf)

            ev_k = pe_acc(pB, [
                (bufk[:, ic * 128:(ic + 1) * 128], xn32_rhs(ic),
                 [sk_ev, xn_ready] if ic == 0 else [])
                for ic in range(NIC)], extra_waits=[pfree[id(pB)]])
            ringA.release(sk_slot, ev_k)
            ev_ke = on("act", lambda h: h.copy(sb_pq[:], pB[:]),
                       waits=[ev_k, ev_fb_q], inc="act")
            pfree[id(pB)] = rd(pB, ev_ke)
            ev_fa_k = on("pe", lambda h: mm(h, pC[:], fmw["fk1_t"][:], sb_pq[:], True, True),
                         waits=[ev_ke, pfree[id(pC)], pread[id(pC)]], inc="pe")
            ev_fb_k = on("pe", lambda h: mm(h, pD[:], fmw["fk2_t"][:], sb_pq[:], True, True),
                         waits=[pfree[id(pD)], pread[id(pD)]], inc="pe")
            e1 = on("dve", lambda h: h.tensor_scalar(
                sb_fa[:], pC[:], fmb["fk1_b"][:, 0:1], None, ALU.add),
                waits=[ev_fa_k, ev_qf, sb_fa_free, sb_fa_free_act], inc="dve")
            pfree[id(pC)] = rd(pC, e1)
            e2 = on("dve", lambda h: h.tensor_scalar(
                sb_fb2[:], pD[:], fmb["fk2_b"][:, 0:1], None, ALU.add),
                waits=[ev_fb_k, ev_qf], inc="dve")
            pfree[id(pD)] = rd(pD, e2)
            ev_kf = on("dve", lambda h: h.tensor_mul(sb_kf[:], sb_fa[:], sb_fb2[:]),
                       waits=[e1, e2], inc="dve")
            if not own:
                sb_fa_free = ev_kf
                sb_fa_free_act = None
            if t == N_PRE and hh == H - 1:
                tap("kf", sb_kf[:], ev_kf)

            sv_slot, sv_ev = ringA.load(lambda h, sl, hh=hh: h.dma_start(
                out=ringA_f[sl].rearrange("p (ic m) -> p ic m", ic=NIC),
                in_=slab16(d_wv, hh * 128)))
            bufv = ringA_f[sv_slot]
            ev_v = pe_acc(pA, [
                (bufv[:, ic * 128:(ic + 1) * 128], xn32_rhs(ic),
                 [sv_ev] if ic == 0 else [])
                for ic in range(NIC)], extra_waits=[pfree[id(pA)]])
            ringA.release(sv_slot, ev_v)
            ev_ve = on("act", lambda h: h.copy(sb_pv[:], pA[:]), waits=[ev_v, ev_kf], inc="act")
            pfree[id(pA)] = rd(pA, ev_ve)

            kt_evs, vt_evs = [], []
            for n in (range(NG) if "attn_trk" in DEV_PHASES else []):
                e_t = on("pe", lambda h, n=n: h.transpose(
                    pE[:, n * 128:(n + 1) * 128], sb_kf[:, n * 128:(n + 1) * 128], ident[:]),
                    waits=[ev_kf, pread[id(pE)]] + ([pfree[id(pE)]] if n == 0 else []),
                    inc="pe")
                e_c = on("dve", lambda h, n=n: h.tensor_copy(
                    sb_ktok[:, n * 128:(n + 1) * 128], pE[:, n * 128:(n + 1) * 128]),
                    waits=[e_t], inc="dve")
                rd(pE, e_c)
                kt_evs.append(e_c)
            if kt_evs:
                pfree[id(pE)] = kt_evs[-1]
            for n in (range(NG) if "attn_trv" in DEV_PHASES else []):
                e_t = on("pe", lambda h, n=n: h.transpose(
                    pF[:, n * 128:(n + 1) * 128], sb_pv[:, n * 128:(n + 1) * 128], ident[:]),
                    waits=[ev_ve, pread[id(pF)]] + ([pfree[id(pF)]] if n == 0 else []),
                    inc="pe")
                e_c = on("dve", lambda h, n=n: h.tensor_copy(
                    sb_vtok[:, n * 128:(n + 1) * 128], pF[:, n * 128:(n + 1) * 128]),
                    waits=[e_t], inc="dve")
                rd(pF, e_c)
                vt_evs.append(e_c)
            if vt_evs:
                pfree[id(pF)] = vt_evs[-1]

            last_oh = None
            last_ol = None
            for n in (range(NG) if "attn_scan" in DEV_PHASES else []):
                cs = slice(n * 128, (n + 1) * 128)
                if own:
                    e_sT = on("pe", lambda h, cs=cs: mm(
                        h, pG[:, cs], sb_kf[:, cs], sb_qf[:, cs], True, True),
                        waits=[ev_kf, ev_qf, pread[id(pG)]]
                              + ([pfree[id(pG)]] if n == 0 else []), inc="pe")
                    e_sm = on("dve", lambda h, cs=cs: h.tensor_mul(
                        sb_sm[:, cs], pG[:, cs], tril[:]), waits=[e_sT], inc="dve")
                    rd(pG, e_sm)
                    e_int = on("pe", lambda h, cs=cs, hh=hh: mm(
                        h, pH[:, cs], sb_S[:, hh * 128:(hh + 1) * 128], sb_qf[:, cs],
                        True, False),
                        waits=[S_up_ev[hh], ev_qf, pread[id(pH)]]
                              + ([pfree[id(pH)]] if n == 0 else []),
                        inc="pe")
                    last_S_user[hh] = e_int
                    e_o = on("pe", lambda h, cs=cs: mm(
                        h, pH[:, cs], sb_vtok[:, cs], sb_sm[:, cs], False, True),
                        waits=[e_sm, vt_evs[n]], inc="pe")
                e_kv = on("pe", lambda h, cs=cs: mm(
                    h, pE[:, cs], sb_ktok[:, cs], sb_vtok[:, cs], True, True),
                    waits=[kt_evs[n], vt_evs[n], kt_evs[-1], pread[id(pE)]], inc="pe")
                e_Sup = on("dve", lambda h, cs=cs, hh=hh: h.tensor_add(
                    sb_S[:, hh * 128:(hh + 1) * 128], sb_S[:, hh * 128:(hh + 1) * 128],
                    pE[:, cs]),
                    waits=[e_kv, last_S_user[hh]], inc="dve")
                rd(pE, e_Sup)
                S_up_ev[hh] = e_Sup
                if n == NG - 1:
                    pfree[id(pE)] = e_Sup

                if own:
                    e_oc = on("dve", lambda h, cs=cs: h.tensor_copy(sb_o[:, cs], pH[:, cs]),
                              waits=[e_o, ev_kf], inc="dve")
                    rd(pH, e_oc)
                    e_osq = on("act", lambda h, cs=cs: h.activation(
                        sb_osq[:, cs], pH[:, cs], AF.Square), waits=[e_o], inc="act")
                    rd(pH, e_osq)
                    e_ss = on("pe", lambda h, cs=cs: mm(
                        h, pG[0:1, cs], ones_col[:], sb_osq[:, cs], True, True),
                        waits=[e_osq, e_sm, pread[id(pG)]], inc="pe")
                    e_st = on("dve", lambda h, cs=cs: h.tensor_scalar(
                        sb_stat[:, cs], pG[0:1, cs], INV_DK, EPS, ALU.mult, ALU.add),
                        waits=[e_ss], inc="dve")
                    rd(pG, e_st)
                    e_sd = on("act", lambda h, cs=cs: h.activation(
                        sb_stat[:, cs], sb_stat[:, cs], AF.Sqrt), waits=[e_st], inc="act")
                    e_rc = on("dve", lambda h, cs=cs: h.reciprocal(
                        sb_stat[:, cs], sb_stat[:, cs]), waits=[e_sd], inc="dve")
                    e_bc = on("pe", lambda h, cs=cs: mm(
                        h, pF[:, cs], ones_row[:], sb_stat[0:1, cs], True, True),
                        waits=[e_rc, vt_evs[-1], pread[id(pF)]], inc="pe")
                    e_or = on("dve", lambda h, cs=cs: h.scalar_tensor_tensor(
                        sb_pq[:, cs], sb_o[:, cs], normw[:, 0:1], pF[:, cs],
                        ALU.mult, ALU.mult),
                        waits=[e_oc, e_bc], inc="dve")
                    rd(pF, e_or)
                    osl = slice(hh * TT + n * 128, hh * TT + (n + 1) * 128)
                    e_oh2 = on("act", lambda h, cs=cs, osl=osl: h.copy(
                        sb_oh[:, osl], sb_pq[:, cs]),
                        waits=[e_or,
                               orms_free_ev if (hh == 0 and n == 0) else None]
                              + (oh_guard if (t == N_PRE and hh == 0 and n == 0) else []),
                        inc="act")
                    e_ol2 = on("dve", lambda h, cs=cs, osl=osl: h.tensor_sub(
                        sb_ol[:, osl], sb_pq[:, cs], sb_oh[:, osl]),
                        waits=[e_or, e_oh2],
                        inc="dve")
                    last_oh = e_oh2
                    last_ol = e_ol2
                    if n == NG - 1:
                        pfree[id(pF)] = e_or
                        pfree[id(pH)] = e_or
                        pfree[id(pG)] = e_st
                        sb_fa_free = e_or
                        sb_fa_free_act = None
            if not own:
                pfree[id(pE)] = S_up_ev[hh]
                pfree[id(pF)] = vt_evs[-1]
                sb_fa_free = ev_kf
                sb_fa_free_act = None
                pq_guard = [None, None]
            else:
                pq_guard = [last_ol, last_oh]

        last_orms_write = S.now("dve")
        last_orms_write_act = S.now("act")

        acc_guard = None
        if own:
            last_wo_mm = None
            for oc in (range(NIC) if ("wo" in DEV_PHASES and "attn" in DEV_PHASES) else []):
                so_slot, so_ev = slab_load(ringA, s_wo, oc, NIC, ready_all)
                bufo = ringA_b[so_slot]
                pW = pA if oc % 2 == 0 else pB
                ev_w = pe_acc3(pW, bufo, H,
                               lambda hh2: sb_oh[:, hh2 * TT:(hh2 + 1) * TT],
                               lambda hh2: sb_ol[:, hh2 * TT:(hh2 + 1) * TT],
                               lambda hh2: ([so_ev, last_orms_write, last_orms_write_act]
                                            if hh2 == 0 else []),
                               extra_waits=[pfree[id(pW)]])
                ringA.release(so_slot, ev_w)
                last_wo_mm = ev_w
                ev_e = on("dve", lambda h, oc=oc, pW=pW: h.tensor_copy(
                    sb_acc[:, oc * TT:(oc + 1) * TT], pW[:]),
                    waits=[ev_w], inc="dve")
                pfree[id(pW)] = rd(pW, ev_e)
            orms_free_ev = last_wo_mm
            if t == N_PRE:
                acc_guard = tap("acc0", sb_acc[:, 0:TT], S.now("dve"))

            rec_bufs = [sb_pq, sb_qf]
            prev_rmm = [None, None]
            qf_pq_last = S.now("pe")
            ev_r = None
            for ic in range(NIC):
                rb = rec_bufs[ic % 2]
                e_rec = on("dve", lambda h, rb=rb, ic=ic: h.tensor_add(
                    rb[:], sb_xnh[:, ic * TT:(ic + 1) * TT],
                    sb_xnl[:, ic * TT:(ic + 1) * TT]),
                    waits=[prev_rmm[ic % 2],
                           qf_pq_last if ic < 2 else None,
                           last_ol if ic == 0 else None,
                           last_oh if ic == 0 else None], inc="dve")
                ev_r = on("pe", lambda h, rb=rb, ic=ic: mm(
                    h, pC[0:NE, :], routerw[:, ic * NE:(ic + 1) * NE], rb[:],
                    ic == 0, ic == NIC - 1),
                    waits=[e_rec] + ([pfree[id(pC)], pread[id(pC)]] if ic == 0 else []),
                    inc="pe")
                prev_rmm[ic % 2] = ev_r
            ev_ze = on("act", lambda h: h.copy(sb_z[:], pC[0:NE, :]), waits=[ev_r], inc="act")
            pfree[id(pC)] = rd(pC, ev_ze)
            last_sg = None
            comb_t_last = None
            prev_cb = None
            for g in range(NG):
                gs = slice(g * 128, (g + 1) * 128)
                e_t = on("pe", lambda h, gs=gs, g=g: mm(
                    h, pE[:, g * 128: g * 128 + NE], sb_z[:, gs], ident[0:NE, 0:NE],
                    True, True),
                    waits=[ev_ze, pread[id(pE)]] + ([pfree[id(pE)]] if g == 0 else []),
                    inc="pe")
                e_sg = on("act", lambda h, g=g: h.activation(
                    sb_sc[:, g * NE:(g + 1) * NE], pE[:, g * 128: g * 128 + NE], AF.Sigmoid),
                    waits=[e_t], inc="act")
                rd(pE, e_sg)
                last_sg = e_sg
                e_b = on("dve", lambda h, g=g: h.tensor_add(
                    sb_bia[:], sb_sc[:, g * NE:(g + 1) * NE], ebbc[:]),
                    waits=[e_sg, ev_ebbc, prev_cb], inc="dve")
                e_m8 = on("dve", lambda h: h.max(out=sb_top[:], in_=sb_bia[:]),
                          waits=[e_b], inc="dve")
                e_mk = on("dve", lambda h: h.tensor_scalar(
                    sb_msk[:], sb_bia[:], sb_top[:, 3:4], None, ALU.is_ge),
                    waits=[e_m8], inc="dve")
                e_w4 = on("dve", lambda h, g=g: h.tensor_mul(
                    sb_w4[:], sb_msk[:], sb_sc[:, g * NE:(g + 1) * NE]),
                    waits=[e_mk], inc="dve")
                e_s4 = on("dve", lambda h: h.tensor_reduce(
                    sb_s4[:], sb_w4[:], AX.X, ALU.add),
                    waits=[e_w4], inc="dve")
                e_s4b = on("dve", lambda h: h.tensor_scalar(
                    sb_s4[:], sb_s4[:], 1e-20, None, ALU.add), waits=[e_s4], inc="dve")
                e_rc = on("dve", lambda h: h.reciprocal(sb_s4[:], sb_s4[:]),
                          waits=[e_s4b], inc="dve")
                e_cb = on("dve", lambda h: h.tensor_scalar(
                    sb_w4[:], sb_w4[:], sb_s4[:, 0:1], None, ALU.mult),
                    waits=[e_rc], inc="dve")
                prev_cb = e_cb
                e_ct = on("pe", lambda h, g=g: mm(
                    h, pC[0:NE, g * 128:(g + 1) * 128], sb_w4[:], ident[:], True, True),
                    waits=[e_cb, pread[id(pC)]] + ([pfree[id(pC)]] if g == 0 else []),
                    inc="pe")
                comb_t_last = e_ct
            pfree[id(pE)] = last_sg
            ev_comb = on("dve", lambda h: h.tensor_copy(sb_comb[:], pC[0:NE, :]),
                         waits=[comb_t_last], inc="dve")
            if t == N_PRE:
                tap("comb", sb_comb[:], ev_comb)
            pfree[id(pC)] = rd(pC, ev_comb)

            cbc_free = ev_comb
            last_acc_add = None
            last_h0 = None
            for e in (range(NE + 1) if "experts" in DEV_PHASES else []):
                routed = e < NE
                ev_cbc = None
                if routed:
                    e_rx = on("pe", lambda h, e=e: mm(
                        h, pD[0:1, :], ident[0:NE, e:e + 1], sb_comb[:], True, True),
                        waits=[ev_comb, pfree[id(pD)], pread[id(pD)], cbc_free], inc="pe")
                    e_rs = on("dve", lambda h: h.tensor_copy(sb_stat[0:1, :], pD[0:1, :]),
                              waits=[e_rx], inc="dve")
                    rd(pD, e_rs)
                    ev_cbc = on("pe", lambda h: mm(
                        h, pD[:], ones_row[:], sb_stat[0:1, :], True, True),
                        waits=[e_rs, pread[id(pD)]], inc="pe")
                h_evs = []
                for j in range(NJ):
                    if routed:
                        gslot, g_ev = slab_load(ringA, s_gu, e * 2 * NJ + 2 * j, NIC, ready_all)
                        uslot, u_ev = slab_load(ringA, s_gu, e * 2 * NJ + 2 * j + 1, NIC, ready_all)
                    else:
                        gslot, g_ev = slab_load(ringA, s_sg, j, NIC, ready_all)
                        uslot, u_ev = slab_load(ringA, s_su, j, NIC, ready_all)
                    bufg, bufu = ringA_b[gslot], ringA_b[uslot]
                    ev_g = pe_acc3(pA, bufg, NIC, xnh_rhs, xnl_rhs,
                                   lambda ic: [g_ev] if ic == 0 else [],
                                   extra_waits=[pfree[id(pA)]])
                    ringA.release(gslot, ev_g)
                    ev_u = pe_acc3(pB, bufu, NIC, xnh_rhs, xnl_rhs,
                                   lambda ic: [u_ev] if ic == 0 else [],
                                   extra_waits=[pfree[id(pB)]])
                    ringA.release(uslot, ev_u)
                    e_sil = on("act", lambda h: h.activation(sb_g[:], pA[:], AF.Silu),
                               waits=[ev_g, last_h0], inc="act")
                    pfree[id(pA)] = rd(pA, e_sil)
                    e_ue = on("act", lambda h: h.copy(sb_u[:], pB[:]),
                              waits=[ev_u, last_h0], inc="act")
                    pfree[id(pB)] = rd(pB, e_ue)
                    e_h0 = on("dve", lambda h, j=j: h.tensor_mul(
                        sb_h[:, j * TT:(j + 1) * TT], sb_g[:], sb_u[:]),
                        waits=[e_sil, e_ue], inc="dve")
                    last_h0 = e_h0
                    if routed:
                        e_h = on("dve", lambda h, j=j: h.tensor_mul(
                            sb_h[:, j * TT:(j + 1) * TT],
                            sb_h[:, j * TT:(j + 1) * TT], pD[:]),
                            waits=[e_h0, ev_cbc], inc="dve")
                        rd(pD, e_h)
                    else:
                        e_h = e_h0
                    h_evs.append(e_h)
                if routed:
                    cbc_free = h_evs[-1]
                    pfree[id(pD)] = h_evs[-1]
                for oc in range(NIC):
                    if routed:
                        dslot, d_ev = ringB.load(lambda h, sl, e=e, oc=oc: h.dma_start(
                            out=ringB_bufs[sl].rearrange("p (ic m) -> p ic m", ic=NJ),
                            in_=slab16(d_dw[e], oc * 128)))
                    else:
                        dslot, d_ev = ringB.load(lambda h, sl, oc=oc: h.dma_start(
                            out=ringB_bufs[sl].rearrange("p (ic m) -> p ic m", ic=NJ),
                            in_=slab16(d_sd, oc * 128)))
                    bufd = ringB_bufs[dslot]
                    pW = pE if oc % 2 == 0 else pF
                    ev_d = pe_acc(pW, [
                        (bufd[:, jc * 128:(jc + 1) * 128], sb_h[:, jc * TT:(jc + 1) * TT],
                         [d_ev, h_evs[-1]] if jc == 0 else [])
                        for jc in range(NJ)], extra_waits=[pfree[id(pW)]])
                    ringB.release(dslot, ev_d)
                    ev_a = on("dve", lambda h, oc=oc, pW=pW: h.tensor_add(
                        sb_acc[:, oc * TT:(oc + 1) * TT],
                        sb_acc[:, oc * TT:(oc + 1) * TT], pW[:]),
                        waits=[ev_d, acc_guard], inc="dve")
                    pfree[id(pW)] = rd(pW, ev_a)
                    last_acc_add = ev_a
                    h_free_ev = ev_d

            last_acc_read = None
            for tg in range(NG):
                r0x = t * TT + tg * 128
                x_ev = on("gp", dma(sb_x[:], d_x[r0x:r0x + 128, :]),
                          waits=[x_free_ev], inc="sX", amt=16)
                st_evs = []
                for oc in range(NIC):
                    pT = pG if oc % 2 == 0 else pH
                    sl = (oc // 2) % 4
                    e_t = on("pe", lambda h, pT=pT, sl=sl, oc=oc, tg=tg: h.transpose(
                        pT[:, sl * 128:(sl + 1) * 128],
                        sb_acc[:, oc * TT + tg * 128: oc * TT + (tg + 1) * 128], ident[:]),
                        waits=[pfree[id(pT)], pread[id(pT)],
                               last_acc_add if (tg == 0 and oc == 0) else None],
                        inc="pe")
                    last_acc_read = e_t
                    e_a = on("dve", lambda h, pT=pT, sl=sl, oc=oc: h.tensor_add(
                        sb_out[:, oc * 128:(oc + 1) * 128],
                        pT[:, sl * 128:(sl + 1) * 128],
                        sb_x[:, oc * 128:(oc + 1) * 128]),
                        waits=[e_t, x_ev, sbout_free_ev if oc == 0 else None], inc="dve")
                    pfree[id(pT)] = rd(pT, e_a)
                    st_evs.append(e_a)
                x_free_ev = st_evs[-1]
                r0 = (t - N_PRE) * TT + tg * 128
                ev_st = on("gp", dma(d_out[r0:r0 + 128, :], sb_out[:]),
                           waits=[st_evs[-1]], inc="sO", amt=16)
                sbout_free_ev = ev_st
            acc_free_ev = last_acc_read
            acc_free_ev2 = None
            xn_free_ev = S.now("pe")
        else:
            xn_free_ev = S.now("pe")
            acc_free_ev = S.now("pe")
            acc_free_ev2 = None

    total_sO = S.counts.get("sO", 0)
    on("gp", lambda h: h.nop(), waits=[("sO", total_sO)])

    sem_names = (["pe", "dve", "act", "gp", "sBoot", "sA0", "sA1",
                  "sB0", "sB1", "sX", "sO", "sTap"]
                 + [f"sPin{i}" for i in range(4)]
                 + [f"sPoh{i}" for i in range(4)]
                 + [f"sPol{i}" for i in range(4)])
    import contextlib
    with contextlib.ExitStack() as stack:
        semmap = {n: stack.enter_context(nc.semaphore("sem_" + n)) for n in sem_names}
        block = stack.enter_context(nc.Block())

        def make_runner(name):
            def run(h):
                waited = {}
                for waits, emit, inc in S.items[name]:
                    neww = []
                    for (sem, val) in waits:
                        if val > waited.get(sem, 0):
                            neww.append((sem, val))
                            waited[sem] = val
                    for (sem, val) in neww[:-1]:
                        h.wait_ge(semmap[sem], val)
                    inst = emit(h)
                    if neww:
                        sem, val = neww[-1]
                        try:
                            inst._wait_ge(semmap[sem], val)
                        except Exception:
                            raise
                    if inc is not None:
                        inst.then_inc(semmap[inc[0]], inc[1])
            return run

        block.tensor(make_runner("pe"))
        block.vector(make_runner("dve"))
        block.scalar(make_runner("act"))
        block.sync(make_runner("sync"))
        block.gpsimd(make_runner("gp"))

    return nc


def _prep_inputs(inputs):
    f = lambda a: np.ascontiguousarray(np.asarray(a, dtype=np.float32))
    x = f(inputs["L0_attn_x"])
    gu = f(inputs["L0_ffn_gate_up_proj"])
    dw = f(inputs["L0_ffn_down_proj"])
    com = {
        "wq_t": f(np.asarray(inputs["L0_attn_wq"]).T),
        "wk_t": f(np.asarray(inputs["L0_attn_wk"]).T),
        "wv_t": f(np.asarray(inputs["L0_attn_wv"]).T),
        "wo_t": f(np.asarray(inputs["L0_attn_wo"]).T),
        "fq1_t": f(np.asarray(inputs["L0_attn_fq1_w"]).T),
        "fq2_t": f(np.asarray(inputs["L0_attn_fq2_w"]).T),
        "fk1_t": f(np.asarray(inputs["L0_attn_fk1_w"]).T),
        "fk2_t": f(np.asarray(inputs["L0_attn_fk2_w"]).T),
        "fq1_b": f(inputs["L0_attn_fq1_b"]).reshape(128, 1),
        "fq2_b": f(inputs["L0_attn_fq2_b"]).reshape(128, 1),
        "fk1_b": f(inputs["L0_attn_fk1_b"]).reshape(128, 1),
        "fk2_b": f(inputs["L0_attn_fk2_b"]).reshape(128, 1),
        "norm_w": f(inputs["L0_attn_norm_w"]).reshape(128, 1),
        "router_wt": f(np.asarray(inputs["L0_ffn_router_w"]).T),
        "e_bias": f(inputs["L0_ffn_e_bias"]).reshape(1, NE),
        "sg_t": f(np.asarray(inputs["L0_ffn_shared_gate_w"]).T),
        "su_t": f(np.asarray(inputs["L0_ffn_shared_up_w"]).T),
        "sd_t": f(np.asarray(inputs["L0_ffn_shared_down_w"]).T),
    }
    gut = np.transpose(gu, (0, 2, 1))
    idx = []
    for j in range(NJ):
        idx.extend(range(j * 128, (j + 1) * 128))
        idx.extend(range(768 + j * 128, 768 + (j + 1) * 128))
    com["gu_t"] = f(gut[:, :, idx])
    com["dw_t"] = f(np.transpose(dw, (0, 2, 1)))

    in_maps = []
    for c in range(8):
        b, q = c // 4, c % 4
        own = x[b, q * 2048:(q + 1) * 2048]
        pre = x[b, :q * 2048]
        pad = np.zeros((N_PRE * TT - q * 2048, D), np.float32)
        xc = np.concatenate([pad, pre, own], axis=0)
        m = dict(com)
        m["x"] = np.ascontiguousarray(xc)
        in_maps.append(m)
    return in_maps


def kernel(**inputs):
    in_maps = _prep_inputs(inputs)
    nc = build_nc()
    res = run_bass_kernel_spmd(nc, in_maps, core_ids=list(range(8)))
    out = np.empty((2, 8192, 2048), np.float32)
    for c in range(8):
        b, q = c // 4, c % 4
        out[b, q * 2048:(q + 1) * 2048] = res.results[c]["out"]
    return out


# version 07
# speedup vs baseline: 1.0040x; 1.0012x over previous
import numpy as np
import concourse.bass as bass
import concourse.mybir as mybir
from concourse.bass_utils import run_bass_kernel_spmd

F32 = mybir.dt.float32
BF16 = mybir.dt.bfloat16
AF = mybir.ActivationFunctionType
ALU = mybir.AluOpType
AX = mybir.AxisListType

D = 2048
H = 16
DK = 128
NIC = D // 128
TT = 512
NG = TT // 128
N_PRE = 12
N_OWN = 4
N_TILE = N_PRE + N_OWN
NE = 16
NJ = 6
EPS = 1e-5
SCALE = float(DK) ** -0.5
INV_D = 1.0 / D
INV_DK = 1.0 / DK
DEV_PHASES = {"attn", "attn_trk", "attn_trv", "attn_scan", "wo", "router", "experts", "epi"}
DEV_TAPS = set()
DEV_SKIP = set()


class Sched:
    ENG = ("pe", "dve", "act", "sync", "gp")

    def __init__(self):
        self.items = {e: [] for e in self.ENG}
        self.counts = {}

    def add(self, eng, emit, waits=(), inc=None, amt=1):
        ev = None
        if inc is not None:
            self.counts[inc] = self.counts.get(inc, 0) + amt
            ev = (inc, self.counts[inc])
        self.items[eng].append(
            (tuple(w for w in waits if w is not None), emit, (inc, amt) if inc else None)
        )
        return ev

    def now(self, sem):
        return (sem, self.counts.get(sem, 0))


class Ring:

    def __init__(self, S, queue, sem_base, bufs):
        self.S, self.q, self.bufs = S, queue, bufs
        self.n = len(bufs)
        self.sems = [f"{sem_base}{i}" for i in range(self.n)]
        self.loads = 0
        self.free_ev = [None] * self.n

    def load(self, dma_fn, extra_waits=()):
        slot = self.loads % self.n
        w = [self.free_ev[slot]] + list(extra_waits)
        ev = self.S.add(
            self.q, (lambda h, sl=slot, f=dma_fn: f(h, sl)),
            waits=w, inc=self.sems[slot], amt=16,
        )
        self.loads += 1
        return slot, ev

    def release(self, slot, ev):
        self.free_ev[slot] = ev


def build_nc():
    nc = bass.Bass("TRN2", target_bir_lowering=False, debug=False)

    def dram_in(name, shape):
        return nc.dram_tensor(name, shape, F32, kind="ExternalInput").ap()

    d_x = dram_in("x", [N_TILE * TT, D])
    d_wq = dram_in("wq_t", [D, D])
    d_wk = dram_in("wk_t", [D, D])
    d_wv = dram_in("wv_t", [D, D])
    d_wo = dram_in("wo_t", [D, D])
    d_f = {n: dram_in(n, [128, 128]) for n in ("fq1_t", "fq2_t", "fk1_t", "fk2_t")}
    d_fb = {n: dram_in(n, [128, 1]) for n in ("fq1_b", "fq2_b", "fk1_b", "fk2_b")}
    d_nw = dram_in("norm_w", [128, 1])
    d_rw = dram_in("router_wt", [D, NE])
    d_eb = dram_in("e_bias", [1, NE])
    d_gu = dram_in("gu_t", [NE, D, 2 * NJ * 128])
    d_dw = dram_in("dw_t", [NE, NJ * 128, D])
    d_sg = dram_in("sg_t", [D, NJ * 128])
    d_su = dram_in("su_t", [D, NJ * 128])
    d_sd = dram_in("sd_t", [NJ * 128, D])
    d_out = nc.dram_tensor("out", [N_OWN * TT, D], F32, kind="ExternalOutput").ap()

    def scr(name, nslab, rows):
        return nc.dram_tensor(name, [nslab, 2 * rows, 128], BF16, kind="Internal").ap()

    s_wo = scr("wo_s", NIC, D)
    s_gu = scr("gu_s", NE * 2 * NJ, D)
    s_sg = scr("sg_s", NJ, D)
    s_su = scr("su_s", NJ, D)

    c_ident = nc.inline_tensor(np.eye(128, dtype=np.float32), name="c_ident")
    tril_np = (np.arange(128)[:, None] <= np.arange(128)[None, :]).astype(np.float32)
    c_tril = nc.inline_tensor(tril_np, name="c_tril")
    c_ones_col = nc.inline_tensor(np.ones((128, 1), np.float32), name="c_ones_col")
    c_ones_row = nc.inline_tensor(np.ones((1, 128), np.float32), name="c_ones_row")
    c_scale = nc.inline_tensor(np.full((128, 1), SCALE, np.float32), name="c_scale")

    def sb(name, shape, dt=F32):
        return nc.alloc_sbuf_tensor(name, shape, dt).ap()

    ident = sb("ident", [128, 128])
    tril = sb("tril", [128, 128])
    ones_col = sb("ones_col", [128, 1])
    ones_row = sb("ones_row", [1, 128])
    scale_col = sb("scale_col", [128, 1])
    fmw = {n: sb("s_" + n, [128, 128]) for n in d_f}
    fmb = {n: sb("s_" + n, [128, 1]) for n in d_fb}
    normw = sb("normw", [128, 1])
    routerw = sb("routerw", [128, NIC * NE])
    eb_row = sb("eb_row", [1, NE])
    ebbc = sb("ebbc", [128, NE])

    sb_x = sb("sb_x", [128, D])
    sb_xnh = sb("sb_xnh", [128, NIC * TT], BF16)
    sb_xnl = sb("sb_xnl", [128, NIC * TT], BF16)
    sb_acc = sb("sb_acc", [128, NIC * TT])
    sb_oh = sb("sb_oh", [128, H * TT], BF16)
    _ol_off = (nc.sbuf_base + 31) & ~31
    sb_ol = sb("sb_ol", [128, H * TT], BF16)
    sb_olf = nc.alloc_sbuf_tensor_at("sb_olf", [128, H * TT // 2], F32,
                                     offset=_ol_off).ap()
    sb_S = sb("sb_S", [128, H * 128])

    sb_pq = sb("sb_pq", [128, TT])
    sb_fa = sb("sb_fa", [128, TT])
    sb_fb2 = sb("sb_fb2", [128, TT])
    sb_pv = sb_fb2
    sb_qf = sb("sb_qf", [128, TT])
    sb_kf = sb("sb_kf", [128, TT])
    sb_ktok = sb("sb_ktok", [128, TT])
    sb_vtok = sb("sb_vtok", [128, TT])
    sb_sm = sb("sb_sm", [128, TT])
    sb_o = sb_fa
    sb_osq = sb_sm
    sb_stat = sb("sb_stat", [1, TT])

    sb_g = sb("sb_g", [128, TT])
    sb_u = sb("sb_u", [128, TT])
    sb_h = sb("sb_h", [128, NJ * TT])

    sb_z = sb("sb_z", [NE, TT])
    sb_sc = sb("sb_sc", [128, NG * NE])
    sb_bia = sb("sb_bia", [128, NE])
    sb_top = sb("sb_top", [128, 8])
    sb_msk = sb("sb_msk", [128, NE])
    sb_w4 = sb("sb_w4", [128, NE])
    sb_s4 = sb("sb_s4", [128, 1])
    sb_comb = sb("sb_comb", [NE, TT])

    sb_out = sb("sb_out", [128, D])
    sb_m = sb("sb_m", [128, 1])
    sb_var = sb("sb_var", [128, 1])
    sb_rstd = sb("sb_rstd", [128, 1])
    sb_msq = sb("sb_msq", [128, 1])
    sb_sq = sb("sb_sq", [128, 1])

    ringA_f, ringA_b = [], []
    for i in range(2):
        _off = (nc.sbuf_base + 31) & ~31
        ringA_f.append(sb(f"rA{i}f", [128, NIC * 128]))
        ringA_b.append(nc.alloc_sbuf_tensor_at(
            f"rA{i}b", [128, 2 * NIC * 128], BF16, offset=_off).ap())
    ringB_bufs = [sb(f"rB{i}", [128, NJ * 128]) for i in range(2)]

    pA, pB, pC, pD, pE, pF, pG, pH = (
        nc.alloc_psum_tensor(f"p{i}", [128, TT], F32).ap() for i in range(8))
    pfree = {id(p): None for p in (pA, pB, pC, pD, pE, pF, pG, pH)}
    pread = {id(p): None for p in (pA, pB, pC, pD, pE, pF, pG, pH)}

    def rd(p, ev):
        pread[id(p)] = ev
        return ev

    S = Sched()
    on = S.add

    ringA = Ring(S, "sync", "sA", ringA_f)
    ringB = Ring(S, "gp", "sB", ringB_bufs)

    def dma(out_ap, in_ap):
        return lambda h: h.dma_start(out=out_ap, in_=in_ap)

    tap_prev = [None]

    def tap(name, ap, ev):
        if name not in DEV_TAPS:
            return None
        dt_ = nc.dram_tensor("tap_" + name, list(ap.shape), F32,
                             kind="ExternalOutput").ap()
        tev = on("gp", dma(dt_, ap), waits=[ev, tap_prev[0]], inc="sTap", amt=16)
        tap_prev[0] = tev
        return tev

    def slab16(dram2d, col0):
        return dram2d[:, col0:col0 + 128].rearrange("(ic p) m -> p ic m", p=128)

    def mm(h, out, lhsT, rhs, start, stop):
        return h.matmul(out, lhsT, rhs, start=start, stop=stop)

    def pe_acc(p_ap, pairs, extra_waits=()):
        n = len(pairs)
        ev = None
        for i, (l, r, w) in enumerate(pairs):
            ww = list(w) + (list(extra_waits) + [pread.get(id(p_ap))] if i == 0 else [])
            ev = on("pe",
                    (lambda h, o=p_ap, l_=l, r_=r, s=(i == 0), t=(i == n - 1):
                     mm(h, o[:], l_, r_, s, t)),
                    waits=ww,
                    inc="pe" if i == n - 1 else None)
        return ev

    def pe_acc3(p_ap, buf, nch, rh_fn, rl_fn, ic_waits, extra_waits=(), out_sl=None):
        o = p_ap[:] if out_sl is None else p_ap[:, out_sl]
        total = nch * 3
        idx = 0
        ev = None
        for ic in range(nch):
            lh = buf[:, ic * 128:(ic + 1) * 128]
            ll = buf[:, (nch + ic) * 128:(nch + ic + 1) * 128]
            for l_, r_ in ((lh, rh_fn(ic)), (lh, rl_fn(ic)), (ll, rh_fn(ic))):
                ww = list(ic_waits(ic)) if idx % 3 == 0 else []
                if idx == 0:
                    ww += list(extra_waits) + [pread.get(id(p_ap))]
                ev = on("pe",
                        (lambda h, o=o, l2=l_, r2=r_, s=(idx == 0), t=(idx == total - 1):
                         mm(h, o, l2, r2, s, t)),
                        waits=ww,
                        inc="pe" if idx == total - 1 else None)
                idx += 1
        return ev

    boot_ld = None
    for out_ap, in_ap in [
        (ident[:], c_ident.ap()), (tril[:], c_tril.ap()),
        (ones_col[:], c_ones_col.ap()), (ones_row[:], c_ones_row.ap()),
        (scale_col[:], c_scale.ap()),
        (fmw["fq1_t"][:], d_f["fq1_t"]), (fmw["fq2_t"][:], d_f["fq2_t"]),
        (fmw["fk1_t"][:], d_f["fk1_t"]), (fmw["fk2_t"][:], d_f["fk2_t"]),
        (fmb["fq1_b"][:], d_fb["fq1_b"]), (fmb["fq2_b"][:], d_fb["fq2_b"]),
        (fmb["fk1_b"][:], d_fb["fk1_b"]), (fmb["fk2_b"][:], d_fb["fk2_b"]),
        (normw[:], d_nw),
        (routerw.rearrange("p (ic m) -> p ic m", ic=NIC),
         d_rw.rearrange("(ic p) m -> p ic m", p=128)),
        (eb_row[:], d_eb),
    ]:
        boot_ld = on("sync", dma(out_ap, in_ap), inc="sBoot", amt=16)

    ev_S0 = on("gp", lambda h: h.memset(sb_S[:], 0.0), inc="gp")
    ev = on("pe", lambda h: mm(h, pA[:, 0:NE], ones_row[:], eb_row[:], True, True),
            waits=[boot_ld], inc="pe")
    ev_ebbc = on("dve", lambda h: h.tensor_copy(ebbc[:], pA[:, 0:NE]), waits=[ev], inc="dve")
    pfree[id(pA)] = rd(pA, ev_ebbc)

    pp_items = []
    if "prepass" not in DEV_SKIP:
        for i in range(NIC):
            pp_items.append(("all", d_wo, i * 128, NIC, s_wo, i))
        for i in range(NJ):
            pp_items.append(("all", d_sg, i * 128, NIC, s_sg, i))
        for i in range(NJ):
            pp_items.append(("all", d_su, i * 128, NIC, s_su, i))
        for e in range(NE):
            for j2 in range(2 * NJ):
                pp_items.append(("all", d_gu[e], j2 * 128, NIC, s_gu, e * 2 * NJ + j2))

    pp_free = [[] for _ in range(2)]
    ready_all = {}
    pp_cursor = [0]

    def emit_pp(n_items):
        i0 = pp_cursor[0]
        for i in range(i0, min(i0 + n_items, len(pp_items))):
            grp, src, col0, nch, dst, sidx = pp_items[i]
            slot = i % 2
            w = nch * 128
            stF = sb_olf[:, slot * 2048: slot * 2048 + w]
            stH = sb_oh[:, slot * 4096: slot * 4096 + w]
            stL = sb_oh[:, slot * 4096 + 2048: slot * 4096 + 2048 + w]
            e_in = on("sync", (lambda h, src=src, col0=col0, nch=nch, stF=stF:
                               h.dma_start(
                                   out=stF.rearrange("p (ic m) -> p ic m", ic=nch),
                                   in_=src[:, col0:col0 + 128].rearrange(
                                       "(ic p) m -> p ic m", p=128))),
                      waits=pp_free[slot], inc=f"sPin{slot}", amt=16)
            e_hi = on("act", (lambda h, stH=stH, stF=stF: h.copy(stH, stF)),
                      waits=[e_in], inc="act")
            e_lo = on("dve", (lambda h, stL=stL, stF=stF, stH=stH:
                              h.tensor_sub(stL, stF, stH)),
                      waits=[e_in, e_hi], inc="dve")
            e_oh = on("gp", (lambda h, dst=dst, sidx=sidx, nch=nch, stH=stH:
                             h.dma_start(
                                 out=dst[sidx, 0:nch * 128, :].rearrange(
                                     "(ic p) m -> p ic m", p=128),
                                 in_=stH.rearrange("p (ic m) -> p ic m", ic=nch))),
                     waits=[e_hi], inc=f"sPoh{slot}", amt=16)
            e_ol = on("gp", (lambda h, dst=dst, sidx=sidx, nch=nch, stL=stL:
                             h.dma_start(
                                 out=dst[sidx, nch * 128:2 * nch * 128, :].rearrange(
                                     "(ic p) m -> p ic m", p=128),
                                 in_=stL.rearrange("p (ic m) -> p ic m", ic=nch))),
                     waits=[e_lo], inc=f"sPol{slot}", amt=16)
            pp_free[slot] = [e_oh, e_ol]
            ready_all[f"h{slot}"] = e_oh
            ready_all[f"l{slot}"] = e_ol
        pp_cursor[0] = min(i0 + n_items, len(pp_items))

    last_S_user = {h_: ev_S0 for h_ in range(H)}
    S_up_ev = {h_: ev_S0 for h_ in range(H)}
    x_free_ev = None
    xn_free_ev = None
    sbout_free_ev = None
    orms_free_ev = None
    acc_free_ev = None
    acc_free_ev2 = None
    def ready_evs():
        return list(ready_all.values())
    h_free_ev = None
    sb_fa_free = None
    sb_fa_free_act = None

    def xnh_rhs(ic):
        return sb_xnh[:, ic * TT:(ic + 1) * TT]

    def xnl_rhs(ic):
        return sb_xnl[:, ic * TT:(ic + 1) * TT]

    def xn32_rhs(ic):
        return sb_acc[:, ic * TT:(ic + 1) * TT]

    def slab_load(ring, dst3, sidx, nch, ready):
        return ring.load(
            (lambda h, sl, d=dst3, s=sidx, nch=nch: h.dma_start(
                out=ringA_b[sl].rearrange("p (hl ic m) -> p hl ic m", hl=2, ic=nch),
                in_=d[s].rearrange("(hl ic p) m -> p hl ic m", hl=2, p=128))),
            extra_waits=ready)

    pp_per_tile = (len(pp_items) + min(N_PRE, 8) - 1) // max(min(N_PRE, 8), 1)
    for t in range(N_TILE):
        own = t >= N_PRE
        if t < N_PRE:
            emit_pp(pp_per_tile)
        elif pp_cursor[0] < len(pp_items):
            emit_pp(len(pp_items))

        ln_last = None
        prev_var_reader = None
        prev_xn = None
        for tg in range(NG):
            r0 = t * TT + tg * 128
            x_ev = on("gp", dma(sb_x[:], d_x[r0:r0 + 128, :]),
                      waits=[x_free_ev], inc="sX", amt=16)
            xg = sb_x[:]
            e_sum = on("dve", lambda h, xg=xg: h.tensor_reduce(sb_m[:], xg, AX.X, ALU.add),
                       waits=[x_ev, prev_xn], inc="dve")
            e_sq = on("act", lambda h, xg=xg: h.activation(
                sb_h[:, 0:D], xg, AF.Square, accum_out=sb_sq[:]),
                waits=[x_ev, h_free_ev, prev_var_reader], inc="act")
            e_m = on("dve", lambda h: h.tensor_scalar(sb_m[:], sb_m[:], INV_D, None, ALU.mult),
                     waits=[e_sum], inc="dve")
            e_va = on("dve", lambda h: h.tensor_scalar(sb_var[:], sb_sq[:], INV_D, None, ALU.mult),
                      waits=[e_sq, e_m], inc="dve")
            e_mq = on("dve", lambda h: h.tensor_mul(sb_msq[:], sb_m[:], sb_m[:]),
                      waits=[e_m], inc="dve")
            e_vb = on("dve", lambda h: h.tensor_sub(sb_var[:], sb_var[:], sb_msq[:]),
                      waits=[e_va, e_mq], inc="dve")
            e_vc = on("dve", lambda h: h.tensor_scalar(sb_var[:], sb_var[:], EPS, None, ALU.add),
                      waits=[e_vb], inc="dve")
            e_sd = on("act", lambda h: h.activation(sb_var[:], sb_var[:], AF.Sqrt),
                      waits=[e_vc], inc="act")
            e_r = on("dve", lambda h: h.reciprocal(sb_rstd[:], sb_var[:]),
                     waits=[e_sd], inc="dve")
            prev_var_reader = e_r
            e_xn = on("dve", lambda h, xg=xg: h.tensor_scalar(
                sb_out[:], xg, sb_m[:, 0:1], sb_rstd[:, 0:1], ALU.subtract, ALU.mult),
                waits=[e_r, sbout_free_ev], inc="dve")
            ln_last = e_xn
            prev_xn = e_xn
            x_free_ev = e_xn
            tr_last = None
            for ic in range(NIC):
                pT = pG if ic % 2 == 0 else pH
                sl = (ic // 2) % 4
                csl = slice(ic * TT + tg * 128, ic * TT + (tg + 1) * 128)
                e_t = on("pe", lambda h, pT=pT, sl=sl, ic=ic: h.transpose(
                    pT[:, sl * 128:(sl + 1) * 128], sb_out[:, ic * 128:(ic + 1) * 128],
                    ident[:]),
                    waits=[e_xn, boot_ld, pfree[id(pT)], pread[id(pT)], xn_free_ev],
                    inc="pe")
                if "c32" not in DEV_SKIP:
                    e_c32 = on("dve", lambda h, pT=pT, sl=sl, csl=csl: h.tensor_copy(
                        sb_acc[:, csl], pT[:, sl * 128:(sl + 1) * 128]),
                        waits=[e_t,
                               acc_free_ev if (tg == 0 and ic == 0) else None,
                               acc_free_ev2 if (tg == 0 and ic == 0) else None],
                        inc="dve")
                    rd(pT, e_c32)
                    last_ev = e_c32
                else:
                    e_c32 = None
                    last_ev = e_t
                if own and "xnsplit" not in DEV_SKIP:
                    e_ch = on("act", lambda h, pT=pT, sl=sl, csl=csl: h.copy(
                        sb_xnh[:, csl], pT[:, sl * 128:(sl + 1) * 128]),
                        waits=[e_t, e_c32], inc="act")
                    rd(pT, e_ch)
                    e_cl = on("dve", lambda h, pT=pT, sl=sl, csl=csl: h.tensor_sub(
                        sb_xnl[:, csl], pT[:, sl * 128:(sl + 1) * 128], sb_xnh[:, csl]),
                        waits=[e_t, e_ch], inc="dve")
                    rd(pT, e_cl)
                    last_ev = e_cl
                pfree[id(pT)] = last_ev
                tr_last = e_t
            sbout_free_ev = tr_last
        xn_ready = S.now("dve")
        xn_ready_act = S.now("act")
        if t == N_PRE:
            tap("xnh", sb_xnh[:], xn_ready)
            tap("xnl", sb_xnl[:], xn_ready)
        xn_free_ev = None

        pq_guard = [None, None]
        last_oh = None
        last_ol = None
        for hh in (range(H) if "attn" in DEV_PHASES else []):
            if own:
                sq_slot, sq_ev = ringA.load(lambda h, sl, hh=hh: h.dma_start(
                    out=ringA_f[sl].rearrange("p (ic m) -> p ic m", ic=NIC),
                    in_=slab16(d_wq, hh * 128)))
                bufq = ringA_f[sq_slot]
            sk_slot, sk_ev = ringA.load(lambda h, sl, hh=hh: h.dma_start(
                out=ringA_f[sl].rearrange("p (ic m) -> p ic m", ic=NIC),
                in_=slab16(d_wk, hh * 128)))
            bufk = ringA_f[sk_slot]

            ev_qf = None
            ev_fb_q = None
            if own:
                ev_q = pe_acc(pA, [
                    (bufq[:, ic * 128:(ic + 1) * 128], xn32_rhs(ic),
                     [sq_ev, xn_ready] if ic == 0 else [])
                    for ic in range(NIC)], extra_waits=[pfree[id(pA)]])
                ringA.release(sq_slot, ev_q)
                ev_qe = on("act", lambda h: h.copy(sb_pq[:], pA[:]),
                           waits=[ev_q, pq_guard[0], pq_guard[1]], inc="act")
                pfree[id(pA)] = rd(pA, ev_qe)
                ev_fa_q = on("pe", lambda h: mm(h, pC[:], fmw["fq1_t"][:], sb_pq[:], True, True),
                             waits=[ev_qe, pfree[id(pC)], pread[id(pC)]], inc="pe")
                ev_fb_q = on("pe", lambda h: mm(h, pD[:], fmw["fq2_t"][:], sb_pq[:], True, True),
                             waits=[pfree[id(pD)], pread[id(pD)]], inc="pe")
                e1 = on("dve", lambda h: h.tensor_scalar(
                    sb_fa[:], pC[:], fmb["fq1_b"][:, 0:1], scale_col[:, 0:1],
                    ALU.add, ALU.mult),
                    waits=[ev_fa_q, sb_fa_free, sb_fa_free_act], inc="dve")
                pfree[id(pC)] = rd(pC, e1)
                e2 = on("dve", lambda h: h.tensor_scalar(
                    sb_fb2[:], pD[:], fmb["fq2_b"][:, 0:1], None, ALU.add),
                    waits=[ev_fb_q], inc="dve")
                pfree[id(pD)] = rd(pD, e2)
                ev_qf = on("dve", lambda h: h.tensor_mul(sb_qf[:], sb_fa[:], sb_fb2[:]),
                           waits=[e1, e2], inc="dve")
                if t == N_PRE and hh == H - 1:
                    tap("qf", sb_qf[:], ev_qf)

            ev_k = pe_acc(pB, [
                (bufk[:, ic * 128:(ic + 1) * 128], xn32_rhs(ic),
                 [sk_ev, xn_ready] if ic == 0 else [])
                for ic in range(NIC)], extra_waits=[pfree[id(pB)]])
            ringA.release(sk_slot, ev_k)
            ev_ke = on("act", lambda h: h.copy(sb_pq[:], pB[:]),
                       waits=[ev_k, ev_fb_q], inc="act")
            pfree[id(pB)] = rd(pB, ev_ke)
            ev_fa_k = on("pe", lambda h: mm(h, pC[:], fmw["fk1_t"][:], sb_pq[:], True, True),
                         waits=[ev_ke, pfree[id(pC)], pread[id(pC)]], inc="pe")
            ev_fb_k = on("pe", lambda h: mm(h, pD[:], fmw["fk2_t"][:], sb_pq[:], True, True),
                         waits=[pfree[id(pD)], pread[id(pD)]], inc="pe")
            e1 = on("dve", lambda h: h.tensor_scalar(
                sb_fa[:], pC[:], fmb["fk1_b"][:, 0:1], None, ALU.add),
                waits=[ev_fa_k, ev_qf, sb_fa_free, sb_fa_free_act], inc="dve")
            pfree[id(pC)] = rd(pC, e1)
            e2 = on("dve", lambda h: h.tensor_scalar(
                sb_fb2[:], pD[:], fmb["fk2_b"][:, 0:1], None, ALU.add),
                waits=[ev_fb_k, ev_qf], inc="dve")
            pfree[id(pD)] = rd(pD, e2)
            ev_kf = on("dve", lambda h: h.tensor_mul(sb_kf[:], sb_fa[:], sb_fb2[:]),
                       waits=[e1, e2], inc="dve")
            if not own:
                sb_fa_free = ev_kf
                sb_fa_free_act = None
            if t == N_PRE and hh == H - 1:
                tap("kf", sb_kf[:], ev_kf)

            sv_slot, sv_ev = ringA.load(lambda h, sl, hh=hh: h.dma_start(
                out=ringA_f[sl].rearrange("p (ic m) -> p ic m", ic=NIC),
                in_=slab16(d_wv, hh * 128)))
            bufv = ringA_f[sv_slot]
            ev_v = pe_acc(pA, [
                (bufv[:, ic * 128:(ic + 1) * 128], xn32_rhs(ic),
                 [sv_ev] if ic == 0 else [])
                for ic in range(NIC)], extra_waits=[pfree[id(pA)]])
            ringA.release(sv_slot, ev_v)
            ev_ve = on("act", lambda h: h.copy(sb_pv[:], pA[:]), waits=[ev_v, ev_kf], inc="act")
            pfree[id(pA)] = rd(pA, ev_ve)

            kt_evs, vt_evs = [], []
            for n in (range(NG) if "attn_trk" in DEV_PHASES else []):
                e_t = on("pe", lambda h, n=n: h.transpose(
                    pE[:, n * 128:(n + 1) * 128], sb_kf[:, n * 128:(n + 1) * 128], ident[:]),
                    waits=[ev_kf, pread[id(pE)]] + ([pfree[id(pE)]] if n == 0 else []),
                    inc="pe")
                e_c = on("dve", lambda h, n=n: h.tensor_copy(
                    sb_ktok[:, n * 128:(n + 1) * 128], pE[:, n * 128:(n + 1) * 128]),
                    waits=[e_t], inc="dve")
                rd(pE, e_c)
                kt_evs.append(e_c)
            if kt_evs:
                pfree[id(pE)] = kt_evs[-1]
            for n in (range(NG) if "attn_trv" in DEV_PHASES else []):
                e_t = on("pe", lambda h, n=n: h.transpose(
                    pF[:, n * 128:(n + 1) * 128], sb_pv[:, n * 128:(n + 1) * 128], ident[:]),
                    waits=[ev_ve, pread[id(pF)]] + ([pfree[id(pF)]] if n == 0 else []),
                    inc="pe")
                e_c = on("dve", lambda h, n=n: h.tensor_copy(
                    sb_vtok[:, n * 128:(n + 1) * 128], pF[:, n * 128:(n + 1) * 128]),
                    waits=[e_t], inc="dve")
                rd(pF, e_c)
                vt_evs.append(e_c)
            if vt_evs:
                pfree[id(pF)] = vt_evs[-1]

            last_oh = None
            last_ol = None
            for n in (range(NG) if "attn_scan" in DEV_PHASES else []):
                cs = slice(n * 128, (n + 1) * 128)
                if own:
                    e_sT = on("pe", lambda h, cs=cs: mm(
                        h, pG[:, cs], sb_kf[:, cs], sb_qf[:, cs], True, True),
                        waits=[ev_kf, ev_qf, pread[id(pG)]]
                              + ([pfree[id(pG)]] if n == 0 else []), inc="pe")
                    e_sm = on("dve", lambda h, cs=cs: h.tensor_mul(
                        sb_sm[:, cs], pG[:, cs], tril[:]), waits=[e_sT], inc="dve")
                    rd(pG, e_sm)
                    e_int = on("pe", lambda h, cs=cs, hh=hh: mm(
                        h, pH[:, cs], sb_S[:, hh * 128:(hh + 1) * 128], sb_qf[:, cs],
                        True, False),
                        waits=[S_up_ev[hh], ev_qf, pread[id(pH)]]
                              + ([pfree[id(pH)]] if n == 0 else []),
                        inc="pe")
                    last_S_user[hh] = e_int
                    e_o = on("pe", lambda h, cs=cs: mm(
                        h, pH[:, cs], sb_vtok[:, cs], sb_sm[:, cs], False, True),
                        waits=[e_sm, vt_evs[n]], inc="pe")
                e_kv = on("pe", lambda h, cs=cs: mm(
                    h, pE[:, cs], sb_ktok[:, cs], sb_vtok[:, cs], True, True),
                    waits=[kt_evs[n], vt_evs[n], kt_evs[-1], pread[id(pE)]], inc="pe")
                e_Sup = on("dve", lambda h, cs=cs, hh=hh: h.tensor_add(
                    sb_S[:, hh * 128:(hh + 1) * 128], sb_S[:, hh * 128:(hh + 1) * 128],
                    pE[:, cs]),
                    waits=[e_kv, last_S_user[hh]], inc="dve")
                rd(pE, e_Sup)
                S_up_ev[hh] = e_Sup
                if n == NG - 1:
                    pfree[id(pE)] = e_Sup

                if own:
                    e_oc = on("dve", lambda h, cs=cs: h.tensor_copy(sb_o[:, cs], pH[:, cs]),
                              waits=[e_o, ev_kf], inc="dve")
                    rd(pH, e_oc)
                    e_osq = on("act", lambda h, cs=cs: h.activation(
                        sb_osq[:, cs], pH[:, cs], AF.Square), waits=[e_o], inc="act")
                    rd(pH, e_osq)
                    e_ss = on("pe", lambda h, cs=cs: mm(
                        h, pG[0:1, cs], ones_col[:], sb_osq[:, cs], True, True),
                        waits=[e_osq, e_sm, pread[id(pG)]], inc="pe")
                    e_st = on("dve", lambda h, cs=cs: h.tensor_scalar(
                        sb_stat[:, cs], pG[0:1, cs], INV_DK, EPS, ALU.mult, ALU.add),
                        waits=[e_ss], inc="dve")
                    rd(pG, e_st)
                    e_sd = on("act", lambda h, cs=cs: h.activation(
                        sb_stat[:, cs], sb_stat[:, cs], AF.Sqrt), waits=[e_st], inc="act")
                    e_rc = on("dve", lambda h, cs=cs: h.reciprocal(
                        sb_stat[:, cs], sb_stat[:, cs]), waits=[e_sd], inc="dve")
                    e_bc = on("pe", lambda h, cs=cs: mm(
                        h, pF[:, cs], ones_row[:], sb_stat[0:1, cs], True, True),
                        waits=[e_rc, vt_evs[-1], pread[id(pF)]], inc="pe")
                    e_or = on("dve", lambda h, cs=cs: h.scalar_tensor_tensor(
                        sb_pq[:, cs], sb_o[:, cs], normw[:, 0:1], pF[:, cs],
                        ALU.mult, ALU.mult),
                        waits=[e_oc, e_bc], inc="dve")
                    rd(pF, e_or)
                    osl = slice(hh * TT + n * 128, hh * TT + (n + 1) * 128)
                    e_oh2 = on("act", lambda h, cs=cs, osl=osl: h.copy(
                        sb_oh[:, osl], sb_pq[:, cs]),
                        waits=[e_or,
                               orms_free_ev if (hh == 0 and n == 0) else None]
                              + (ready_evs() if (t == N_PRE and hh == 0 and n == 0) else []),
                        inc="act")
                    e_ol2 = on("dve", lambda h, cs=cs, osl=osl: h.tensor_sub(
                        sb_ol[:, osl], sb_pq[:, cs], sb_oh[:, osl]),
                        waits=[e_or, e_oh2],
                        inc="dve")
                    last_oh = e_oh2
                    last_ol = e_ol2
                    if n == NG - 1:
                        pfree[id(pF)] = e_or
                        pfree[id(pH)] = e_or
                        pfree[id(pG)] = e_st
                        sb_fa_free = e_or
                        sb_fa_free_act = None
            if not own:
                pfree[id(pE)] = S_up_ev[hh]
                pfree[id(pF)] = vt_evs[-1]
                sb_fa_free = ev_kf
                sb_fa_free_act = None
                pq_guard = [None, None]
            else:
                pq_guard = [last_ol, last_oh]

        last_orms_write = S.now("dve")
        last_orms_write_act = S.now("act")

        acc_guard = None
        if own:
            last_wo_mm = None
            for oc in (range(NIC) if ("wo" in DEV_PHASES and "attn" in DEV_PHASES) else []):
                so_slot, so_ev = slab_load(ringA, s_wo, oc, NIC, ready_evs())
                bufo = ringA_b[so_slot]
                pW = pA if oc % 2 == 0 else pB
                ev_w = pe_acc3(pW, bufo, H,
                               lambda hh2: sb_oh[:, hh2 * TT:(hh2 + 1) * TT],
                               lambda hh2: sb_ol[:, hh2 * TT:(hh2 + 1) * TT],
                               lambda hh2: ([so_ev, last_orms_write, last_orms_write_act]
                                            if hh2 == 0 else []),
                               extra_waits=[pfree[id(pW)]])
                ringA.release(so_slot, ev_w)
                last_wo_mm = ev_w
                ev_e = on("dve", lambda h, oc=oc, pW=pW: h.tensor_copy(
                    sb_acc[:, oc * TT:(oc + 1) * TT], pW[:]),
                    waits=[ev_w], inc="dve")
                pfree[id(pW)] = rd(pW, ev_e)
            orms_free_ev = last_wo_mm
            if t == N_PRE:
                acc_guard = tap("acc0", sb_acc[:, 0:TT], S.now("dve"))

            rec_bufs = [sb_pq, sb_qf]
            prev_rmm = [None, None]
            qf_pq_last = S.now("pe")
            ev_r = None
            for ic in range(NIC):
                rb = rec_bufs[ic % 2]
                e_rec = on("dve", lambda h, rb=rb, ic=ic: h.tensor_add(
                    rb[:], sb_xnh[:, ic * TT:(ic + 1) * TT],
                    sb_xnl[:, ic * TT:(ic + 1) * TT]),
                    waits=[prev_rmm[ic % 2],
                           qf_pq_last if ic < 2 else None,
                           last_ol if ic == 0 else None,
                           last_oh if ic == 0 else None], inc="dve")
                ev_r = on("pe", lambda h, rb=rb, ic=ic: mm(
                    h, pC[0:NE, :], routerw[:, ic * NE:(ic + 1) * NE], rb[:],
                    ic == 0, ic == NIC - 1),
                    waits=[e_rec] + ([pfree[id(pC)], pread[id(pC)]] if ic == 0 else []),
                    inc="pe")
                prev_rmm[ic % 2] = ev_r
            ev_ze = on("act", lambda h: h.copy(sb_z[:], pC[0:NE, :]), waits=[ev_r], inc="act")
            pfree[id(pC)] = rd(pC, ev_ze)
            last_sg = None
            comb_t_last = None
            prev_cb = None
            for g in range(NG):
                gs = slice(g * 128, (g + 1) * 128)
                e_t = on("pe", lambda h, gs=gs, g=g: mm(
                    h, pE[:, g * 128: g * 128 + NE], sb_z[:, gs], ident[0:NE, 0:NE],
                    True, True),
                    waits=[ev_ze, pread[id(pE)]] + ([pfree[id(pE)]] if g == 0 else []),
                    inc="pe")
                e_sg = on("act", lambda h, g=g: h.activation(
                    sb_sc[:, g * NE:(g + 1) * NE], pE[:, g * 128: g * 128 + NE], AF.Sigmoid),
                    waits=[e_t], inc="act")
                rd(pE, e_sg)
                last_sg = e_sg
                e_b = on("dve", lambda h, g=g: h.tensor_add(
                    sb_bia[:], sb_sc[:, g * NE:(g + 1) * NE], ebbc[:]),
                    waits=[e_sg, ev_ebbc, prev_cb], inc="dve")
                e_m8 = on("dve", lambda h: h.max(out=sb_top[:], in_=sb_bia[:]),
                          waits=[e_b], inc="dve")
                e_mk = on("dve", lambda h: h.tensor_scalar(
                    sb_msk[:], sb_bia[:], sb_top[:, 3:4], None, ALU.is_ge),
                    waits=[e_m8], inc="dve")
                e_w4 = on("dve", lambda h, g=g: h.tensor_mul(
                    sb_w4[:], sb_msk[:], sb_sc[:, g * NE:(g + 1) * NE]),
                    waits=[e_mk], inc="dve")
                e_s4 = on("dve", lambda h: h.tensor_reduce(
                    sb_s4[:], sb_w4[:], AX.X, ALU.add),
                    waits=[e_w4], inc="dve")
                e_s4b = on("dve", lambda h: h.tensor_scalar(
                    sb_s4[:], sb_s4[:], 1e-20, None, ALU.add), waits=[e_s4], inc="dve")
                e_rc = on("dve", lambda h: h.reciprocal(sb_s4[:], sb_s4[:]),
                          waits=[e_s4b], inc="dve")
                e_cb = on("dve", lambda h: h.tensor_scalar(
                    sb_w4[:], sb_w4[:], sb_s4[:, 0:1], None, ALU.mult),
                    waits=[e_rc], inc="dve")
                prev_cb = e_cb
                e_ct = on("pe", lambda h, g=g: mm(
                    h, pC[0:NE, g * 128:(g + 1) * 128], sb_w4[:], ident[:], True, True),
                    waits=[e_cb, pread[id(pC)]] + ([pfree[id(pC)]] if g == 0 else []),
                    inc="pe")
                comb_t_last = e_ct
            pfree[id(pE)] = last_sg
            ev_comb = on("dve", lambda h: h.tensor_copy(sb_comb[:], pC[0:NE, :]),
                         waits=[comb_t_last], inc="dve")
            if t == N_PRE:
                tap("comb", sb_comb[:], ev_comb)
            pfree[id(pC)] = rd(pC, ev_comb)

            cbc_free = ev_comb
            last_acc_add = None
            last_h0 = None
            for e in (range(NE + 1) if "experts" in DEV_PHASES else []):
                routed = e < NE
                ev_cbc = None
                if routed:
                    e_rx = on("pe", lambda h, e=e: mm(
                        h, pD[0:1, :], ident[0:NE, e:e + 1], sb_comb[:], True, True),
                        waits=[ev_comb, pfree[id(pD)], pread[id(pD)], cbc_free], inc="pe")
                    e_rs = on("dve", lambda h: h.tensor_copy(sb_stat[0:1, :], pD[0:1, :]),
                              waits=[e_rx], inc="dve")
                    rd(pD, e_rs)
                    ev_cbc = on("pe", lambda h: mm(
                        h, pD[:], ones_row[:], sb_stat[0:1, :], True, True),
                        waits=[e_rs, pread[id(pD)]], inc="pe")
                h_evs = []
                for j in range(NJ):
                    if routed:
                        gslot, g_ev = slab_load(ringA, s_gu, e * 2 * NJ + 2 * j, NIC, ready_evs())
                        uslot, u_ev = slab_load(ringA, s_gu, e * 2 * NJ + 2 * j + 1, NIC, ready_evs())
                    else:
                        gslot, g_ev = slab_load(ringA, s_sg, j, NIC, ready_evs())
                        uslot, u_ev = slab_load(ringA, s_su, j, NIC, ready_evs())
                    bufg, bufu = ringA_b[gslot], ringA_b[uslot]
                    ev_g = pe_acc3(pA, bufg, NIC, xnh_rhs, xnl_rhs,
                                   lambda ic: [g_ev] if ic == 0 else [],
                                   extra_waits=[pfree[id(pA)]])
                    ringA.release(gslot, ev_g)
                    ev_u = pe_acc3(pB, bufu, NIC, xnh_rhs, xnl_rhs,
                                   lambda ic: [u_ev] if ic == 0 else [],
                                   extra_waits=[pfree[id(pB)]])
                    ringA.release(uslot, ev_u)
                    e_sil = on("act", lambda h: h.activation(sb_g[:], pA[:], AF.Silu),
                               waits=[ev_g, last_h0], inc="act")
                    pfree[id(pA)] = rd(pA, e_sil)
                    e_ue = on("act", lambda h: h.copy(sb_u[:], pB[:]),
                              waits=[ev_u, last_h0], inc="act")
                    pfree[id(pB)] = rd(pB, e_ue)
                    e_h0 = on("dve", lambda h, j=j: h.tensor_mul(
                        sb_h[:, j * TT:(j + 1) * TT], sb_g[:], sb_u[:]),
                        waits=[e_sil, e_ue], inc="dve")
                    last_h0 = e_h0
                    if routed:
                        e_h = on("dve", lambda h, j=j: h.tensor_mul(
                            sb_h[:, j * TT:(j + 1) * TT],
                            sb_h[:, j * TT:(j + 1) * TT], pD[:]),
                            waits=[e_h0, ev_cbc], inc="dve")
                        rd(pD, e_h)
                    else:
                        e_h = e_h0
                    h_evs.append(e_h)
                if routed:
                    cbc_free = h_evs[-1]
                    pfree[id(pD)] = h_evs[-1]
                for oc in range(NIC):
                    if routed:
                        dslot, d_ev = ringB.load(lambda h, sl, e=e, oc=oc: h.dma_start(
                            out=ringB_bufs[sl].rearrange("p (ic m) -> p ic m", ic=NJ),
                            in_=slab16(d_dw[e], oc * 128)))
                    else:
                        dslot, d_ev = ringB.load(lambda h, sl, oc=oc: h.dma_start(
                            out=ringB_bufs[sl].rearrange("p (ic m) -> p ic m", ic=NJ),
                            in_=slab16(d_sd, oc * 128)))
                    bufd = ringB_bufs[dslot]
                    pW = pE if oc % 2 == 0 else pF
                    ev_d = pe_acc(pW, [
                        (bufd[:, jc * 128:(jc + 1) * 128], sb_h[:, jc * TT:(jc + 1) * TT],
                         [d_ev, h_evs[-1]] if jc == 0 else [])
                        for jc in range(NJ)], extra_waits=[pfree[id(pW)]])
                    ringB.release(dslot, ev_d)
                    ev_a = on("dve", lambda h, oc=oc, pW=pW: h.tensor_add(
                        sb_acc[:, oc * TT:(oc + 1) * TT],
                        sb_acc[:, oc * TT:(oc + 1) * TT], pW[:]),
                        waits=[ev_d, acc_guard], inc="dve")
                    pfree[id(pW)] = rd(pW, ev_a)
                    last_acc_add = ev_a
                    h_free_ev = ev_d

            last_acc_read = None
            for tg in range(NG):
                r0x = t * TT + tg * 128
                x_ev = on("gp", dma(sb_x[:], d_x[r0x:r0x + 128, :]),
                          waits=[x_free_ev], inc="sX", amt=16)
                st_evs = []
                for oc in range(NIC):
                    pT = pG if oc % 2 == 0 else pH
                    sl = (oc // 2) % 4
                    e_t = on("pe", lambda h, pT=pT, sl=sl, oc=oc, tg=tg: h.transpose(
                        pT[:, sl * 128:(sl + 1) * 128],
                        sb_acc[:, oc * TT + tg * 128: oc * TT + (tg + 1) * 128], ident[:]),
                        waits=[pfree[id(pT)], pread[id(pT)],
                               last_acc_add if (tg == 0 and oc == 0) else None],
                        inc="pe")
                    last_acc_read = e_t
                    e_a = on("dve", lambda h, pT=pT, sl=sl, oc=oc: h.tensor_add(
                        sb_out[:, oc * 128:(oc + 1) * 128],
                        pT[:, sl * 128:(sl + 1) * 128],
                        sb_x[:, oc * 128:(oc + 1) * 128]),
                        waits=[e_t, x_ev, sbout_free_ev if oc == 0 else None], inc="dve")
                    pfree[id(pT)] = rd(pT, e_a)
                    st_evs.append(e_a)
                x_free_ev = st_evs[-1]
                r0 = (t - N_PRE) * TT + tg * 128
                ev_st = on("gp", dma(d_out[r0:r0 + 128, :], sb_out[:]),
                           waits=[st_evs[-1]], inc="sO", amt=16)
                sbout_free_ev = ev_st
            acc_free_ev = last_acc_read
            acc_free_ev2 = None
            xn_free_ev = S.now("pe")
        else:
            xn_free_ev = S.now("pe")
            acc_free_ev = S.now("pe")
            acc_free_ev2 = None

    total_sO = S.counts.get("sO", 0)
    on("gp", lambda h: h.nop(), waits=[("sO", total_sO)])

    sem_names = (["pe", "dve", "act", "gp", "sBoot", "sA0", "sA1",
                  "sB0", "sB1", "sX", "sO", "sTap"]
                 + [f"sPin{i}" for i in range(4)]
                 + [f"sPoh{i}" for i in range(4)]
                 + [f"sPol{i}" for i in range(4)])
    import contextlib
    with contextlib.ExitStack() as stack:
        semmap = {n: stack.enter_context(nc.semaphore("sem_" + n)) for n in sem_names}
        block = stack.enter_context(nc.Block())

        def make_runner(name):
            def run(h):
                waited = {}
                for waits, emit, inc in S.items[name]:
                    neww = []
                    for (sem, val) in waits:
                        if val > waited.get(sem, 0):
                            neww.append((sem, val))
                            waited[sem] = val
                    for (sem, val) in neww[:-1]:
                        h.wait_ge(semmap[sem], val)
                    inst = emit(h)
                    if neww:
                        sem, val = neww[-1]
                        try:
                            inst._wait_ge(semmap[sem], val)
                        except Exception:
                            raise
                    if inc is not None:
                        inst.then_inc(semmap[inc[0]], inc[1])
            return run

        block.tensor(make_runner("pe"))
        block.vector(make_runner("dve"))
        block.scalar(make_runner("act"))
        block.sync(make_runner("sync"))
        block.gpsimd(make_runner("gp"))

    return nc


def _prep_inputs(inputs):
    f = lambda a: np.ascontiguousarray(np.asarray(a, dtype=np.float32))
    x = f(inputs["L0_attn_x"])
    gu = f(inputs["L0_ffn_gate_up_proj"])
    dw = f(inputs["L0_ffn_down_proj"])
    com = {
        "wq_t": f(np.asarray(inputs["L0_attn_wq"]).T),
        "wk_t": f(np.asarray(inputs["L0_attn_wk"]).T),
        "wv_t": f(np.asarray(inputs["L0_attn_wv"]).T),
        "wo_t": f(np.asarray(inputs["L0_attn_wo"]).T),
        "fq1_t": f(np.asarray(inputs["L0_attn_fq1_w"]).T),
        "fq2_t": f(np.asarray(inputs["L0_attn_fq2_w"]).T),
        "fk1_t": f(np.asarray(inputs["L0_attn_fk1_w"]).T),
        "fk2_t": f(np.asarray(inputs["L0_attn_fk2_w"]).T),
        "fq1_b": f(inputs["L0_attn_fq1_b"]).reshape(128, 1),
        "fq2_b": f(inputs["L0_attn_fq2_b"]).reshape(128, 1),
        "fk1_b": f(inputs["L0_attn_fk1_b"]).reshape(128, 1),
        "fk2_b": f(inputs["L0_attn_fk2_b"]).reshape(128, 1),
        "norm_w": f(inputs["L0_attn_norm_w"]).reshape(128, 1),
        "router_wt": f(np.asarray(inputs["L0_ffn_router_w"]).T),
        "e_bias": f(inputs["L0_ffn_e_bias"]).reshape(1, NE),
        "sg_t": f(np.asarray(inputs["L0_ffn_shared_gate_w"]).T),
        "su_t": f(np.asarray(inputs["L0_ffn_shared_up_w"]).T),
        "sd_t": f(np.asarray(inputs["L0_ffn_shared_down_w"]).T),
    }
    gut = np.transpose(gu, (0, 2, 1))
    idx = []
    for j in range(NJ):
        idx.extend(range(j * 128, (j + 1) * 128))
        idx.extend(range(768 + j * 128, 768 + (j + 1) * 128))
    com["gu_t"] = f(gut[:, :, idx])
    com["dw_t"] = f(np.transpose(dw, (0, 2, 1)))

    in_maps = []
    for c in range(8):
        b, q = c // 4, c % 4
        own = x[b, q * 2048:(q + 1) * 2048]
        pre = x[b, :q * 2048]
        pad = np.zeros((N_PRE * TT - q * 2048, D), np.float32)
        xc = np.concatenate([pad, pre, own], axis=0)
        m = dict(com)
        m["x"] = np.ascontiguousarray(xc)
        in_maps.append(m)
    return in_maps


def kernel(**inputs):
    in_maps = _prep_inputs(inputs)
    nc = build_nc()
    res = run_bass_kernel_spmd(nc, in_maps, core_ids=list(range(8)))
    out = np.empty((2, 8192, 2048), np.float32)
    for c in range(8):
        b, q = c // 4, c % 4
        out[b, q * 2048:(q + 1) * 2048] = res.results[c]["out"]
    return out


# version 08
# speedup vs baseline: 1.1143x; 1.1099x over previous
import numpy as np
import concourse.bass as bass
import concourse.mybir as mybir
from concourse.bass_utils import run_bass_kernel_spmd

F32 = mybir.dt.float32
BF16 = mybir.dt.bfloat16
AF = mybir.ActivationFunctionType
ALU = mybir.AluOpType
AX = mybir.AxisListType

D = 2048
H = 16
DK = 128
NIC = D // 128
TT = 512
NG = TT // 128
N_PRE = 12
N_OWN = 4
N_TILE = N_PRE + N_OWN
NE = 16
NJ = 6
EPS = 1e-5
SCALE = float(DK) ** -0.5
INV_D = 1.0 / D
INV_DK = 1.0 / DK
DEV_PHASES = {"attn", "attn_trk", "attn_trv", "attn_scan", "wo", "router", "experts", "epi"}
DEV_TAPS = set()
DEV_SKIP = set()


class Sched:
    ENG = ("pe", "dve", "act", "sync", "gp")

    def __init__(self):
        self.items = {e: [] for e in self.ENG}
        self.counts = {}

    def add(self, eng, emit, waits=(), inc=None, amt=1):
        ev = None
        if inc is not None:
            self.counts[inc] = self.counts.get(inc, 0) + amt
            ev = (inc, self.counts[inc])
        self.items[eng].append(
            (tuple(w for w in waits if w is not None), emit, (inc, amt) if inc else None)
        )
        return ev

    def now(self, sem):
        return (sem, self.counts.get(sem, 0))


class Ring:

    def __init__(self, S, queue, sem_base, bufs):
        self.S, self.q, self.bufs = S, queue, bufs
        self.n = len(bufs)
        self.sems = [f"{sem_base}{i}" for i in range(self.n)]
        self.loads = 0
        self.free_ev = [None] * self.n

    def load(self, dma_fn, extra_waits=()):
        slot = self.loads % self.n
        w = [self.free_ev[slot]] + list(extra_waits)
        ev = self.S.add(
            self.q, (lambda h, sl=slot, f=dma_fn: f(h, sl)),
            waits=w, inc=self.sems[slot], amt=16,
        )
        self.loads += 1
        return slot, ev

    def release(self, slot, ev):
        self.free_ev[slot] = ev


def build_nc():
    nc = bass.Bass("TRN2", target_bir_lowering=False, debug=False)

    def dram_in(name, shape):
        return nc.dram_tensor(name, shape, F32, kind="ExternalInput").ap()

    d_x = dram_in("x", [N_TILE * TT, D])
    d_wq = dram_in("wq_t", [D, D])
    d_wk = dram_in("wk_t", [D, D])
    d_wv = dram_in("wv_t", [D, D])
    d_wo = dram_in("wo_t", [D, D])
    d_f = {n: dram_in(n, [128, 128]) for n in ("fq1_t", "fq2_t", "fk1_t", "fk2_t")}
    d_fb = {n: dram_in(n, [128, 1]) for n in ("fq1_b", "fq2_b", "fk1_b", "fk2_b")}
    d_nw = dram_in("norm_w", [128, 1])
    d_rw = dram_in("router_wt", [D, NE])
    d_eb = dram_in("e_bias", [1, NE])
    d_gu = dram_in("gu_t", [NE, D, 2 * NJ * 128])
    d_dw = dram_in("dw_t", [NE, NJ * 128, D])
    d_sg = dram_in("sg_t", [D, NJ * 128])
    d_su = dram_in("su_t", [D, NJ * 128])
    d_sd = dram_in("sd_t", [NJ * 128, D])
    d_out = nc.dram_tensor("out", [N_OWN * TT, D], F32, kind="ExternalOutput").ap()

    def scr(name, nslab, rows):
        return nc.dram_tensor(name, [nslab, 2 * rows, 128], BF16, kind="Internal").ap()

    s_wo = scr("wo_s", NIC, D)
    s_gu = scr("gu_s", NE * 2 * NJ, D)
    s_sg = scr("sg_s", NJ, D)
    s_su = scr("su_s", NJ, D)

    c_ident = nc.inline_tensor(np.eye(128, dtype=np.float32), name="c_ident")
    tril_np = (np.arange(128)[:, None] <= np.arange(128)[None, :]).astype(np.float32)
    c_tril = nc.inline_tensor(tril_np, name="c_tril")
    c_ones_col = nc.inline_tensor(np.ones((128, 1), np.float32), name="c_ones_col")
    c_ones_row = nc.inline_tensor(np.ones((1, 128), np.float32), name="c_ones_row")
    c_scale = nc.inline_tensor(np.full((128, 1), SCALE, np.float32), name="c_scale")

    def sb(name, shape, dt=F32):
        return nc.alloc_sbuf_tensor(name, shape, dt).ap()

    ident = sb("ident", [128, 128])
    tril = sb("tril", [128, 128])
    ones_col = sb("ones_col", [128, 1])
    ones_row = sb("ones_row", [1, 128])
    scale_col = sb("scale_col", [128, 1])
    fmw = {n: sb("s_" + n, [128, 128]) for n in d_f}
    fmb = {n: sb("s_" + n, [128, 1]) for n in d_fb}
    normw = sb("normw", [128, 1])
    routerw = sb("routerw", [128, NIC * NE])
    eb_row = sb("eb_row", [1, NE])
    ebbc = sb("ebbc", [128, NE])

    sb_x = sb("sb_x", [128, D])
    sb_xnh = sb("sb_xnh", [128, NIC * TT], BF16)
    sb_xnl = sb("sb_xnl", [128, NIC * TT], BF16)
    sb_acc = sb("sb_acc", [128, NIC * TT])
    sb_oh = sb("sb_oh", [128, H * TT], BF16)
    _ol_off = (nc.sbuf_base + 31) & ~31
    sb_ol = sb("sb_ol", [128, H * TT], BF16)
    sb_olf = nc.alloc_sbuf_tensor_at("sb_olf", [128, H * TT // 2], F32,
                                     offset=_ol_off).ap()
    sb_S = sb("sb_S", [128, H * 128])

    sb_pq = sb("sb_pq", [128, TT])
    sb_fa = sb("sb_fa", [128, TT])
    sb_fb2 = sb("sb_fb2", [128, TT])
    sb_pv = sb_fb2
    sb_qf = sb("sb_qf", [128, TT])
    sb_kf = sb("sb_kf", [128, TT])
    sb_ktok = sb("sb_ktok", [128, TT])
    sb_vtok = sb("sb_vtok", [128, TT])
    sb_sm = sb("sb_sm", [128, TT])
    sb_o = sb_fa
    sb_osq = sb_sm
    sb_stat = sb("sb_stat", [1, TT])

    sb_g = sb("sb_g", [128, TT])
    sb_u = sb("sb_u", [128, TT])
    sb_h = sb("sb_h", [128, NJ * TT])

    sb_z = sb("sb_z", [NE, TT])
    sb_sc = sb("sb_sc", [128, NG * NE])
    sb_bia = sb("sb_bia", [128, NE])
    sb_top = sb("sb_top", [128, 8])
    sb_msk = sb("sb_msk", [128, NE])
    sb_w4 = sb("sb_w4", [128, NE])
    sb_s4 = sb("sb_s4", [128, 1])
    sb_comb = sb("sb_comb", [NE, TT])

    sb_out = sb("sb_out", [128, D])
    sb_m = sb("sb_m", [128, 1])
    sb_var = sb("sb_var", [128, 1])
    sb_rstd = sb("sb_rstd", [128, 1])
    sb_msq = sb("sb_msq", [128, 1])
    sb_sq = sb("sb_sq", [128, 1])

    ringA_f, ringA_b = [], []
    for i in range(2):
        _off = (nc.sbuf_base + 31) & ~31
        ringA_f.append(sb(f"rA{i}f", [128, NIC * 128]))
        ringA_b.append(nc.alloc_sbuf_tensor_at(
            f"rA{i}b", [128, 2 * NIC * 128], BF16, offset=_off).ap())
    ringB_bufs = [sb(f"rB{i}", [128, NJ * 128]) for i in range(2)]

    pA, pB, pC, pD, pE, pF, pG, pH = (
        nc.alloc_psum_tensor(f"p{i}", [128, TT], F32).ap() for i in range(8))
    pfree = {id(p): None for p in (pA, pB, pC, pD, pE, pF, pG, pH)}
    pread = {id(p): None for p in (pA, pB, pC, pD, pE, pF, pG, pH)}

    def rd(p, ev):
        pread[id(p)] = ev
        return ev

    S = Sched()
    on = S.add

    ringA = Ring(S, "sync", "sA", ringA_f)
    ringB = Ring(S, "gp", "sB", ringB_bufs)

    def dma(out_ap, in_ap):
        return lambda h: h.dma_start(out=out_ap, in_=in_ap)

    tap_prev = [None]

    def tap(name, ap, ev):
        if name not in DEV_TAPS:
            return None
        dt_ = nc.dram_tensor("tap_" + name, list(ap.shape), F32,
                             kind="ExternalOutput").ap()
        tev = on("gp", dma(dt_, ap), waits=[ev, tap_prev[0]], inc="sTap", amt=16)
        tap_prev[0] = tev
        return tev

    def slab16(dram2d, col0):
        return dram2d[:, col0:col0 + 128].rearrange("(ic p) m -> p ic m", p=128)

    def mm(h, out, lhsT, rhs, start, stop):
        return h.matmul(out, lhsT, rhs, start=start, stop=stop)

    def pe_acc(p_ap, pairs, extra_waits=()):
        n = len(pairs)
        ev = None
        for i, (l, r, w) in enumerate(pairs):
            ww = list(w) + (list(extra_waits) + [pread.get(id(p_ap))] if i == 0 else [])
            ev = on("pe",
                    (lambda h, o=p_ap, l_=l, r_=r, s=(i == 0), t=(i == n - 1):
                     mm(h, o[:], l_, r_, s, t)),
                    waits=ww,
                    inc="pe" if i == n - 1 else None)
        return ev

    def pe_acc3(p_ap, buf, nch, rh_fn, rl_fn, ic_waits, extra_waits=(), out_sl=None):
        o = p_ap[:] if out_sl is None else p_ap[:, out_sl]
        total = nch * 3
        idx = 0
        ev = None
        for ic in range(nch):
            lh = buf[:, ic * 128:(ic + 1) * 128]
            ll = buf[:, (nch + ic) * 128:(nch + ic + 1) * 128]
            for l_, r_ in ((lh, rh_fn(ic)), (lh, rl_fn(ic)), (ll, rh_fn(ic))):
                ww = list(ic_waits(ic)) if idx % 3 == 0 else []
                if idx == 0:
                    ww += list(extra_waits) + [pread.get(id(p_ap))]
                ev = on("pe",
                        (lambda h, o=o, l2=l_, r2=r_, s=(idx == 0), t=(idx == total - 1):
                         mm(h, o, l2, r2, s, t)),
                        waits=ww,
                        inc="pe" if idx == total - 1 else None)
                idx += 1
        return ev

    boot_ld = None
    for out_ap, in_ap in [
        (ident[:], c_ident.ap()), (tril[:], c_tril.ap()),
        (ones_col[:], c_ones_col.ap()), (ones_row[:], c_ones_row.ap()),
        (scale_col[:], c_scale.ap()),
        (fmw["fq1_t"][:], d_f["fq1_t"]), (fmw["fq2_t"][:], d_f["fq2_t"]),
        (fmw["fk1_t"][:], d_f["fk1_t"]), (fmw["fk2_t"][:], d_f["fk2_t"]),
        (fmb["fq1_b"][:], d_fb["fq1_b"]), (fmb["fq2_b"][:], d_fb["fq2_b"]),
        (fmb["fk1_b"][:], d_fb["fk1_b"]), (fmb["fk2_b"][:], d_fb["fk2_b"]),
        (normw[:], d_nw),
        (routerw.rearrange("p (ic m) -> p ic m", ic=NIC),
         d_rw.rearrange("(ic p) m -> p ic m", p=128)),
        (eb_row[:], d_eb),
    ]:
        boot_ld = on("sync", dma(out_ap, in_ap), inc="sBoot", amt=16)

    ev_S0 = on("gp", lambda h: h.memset(sb_S[:], 0.0), inc="gp")
    ev = on("pe", lambda h: mm(h, pA[:, 0:NE], ones_row[:], eb_row[:], True, True),
            waits=[boot_ld], inc="pe")
    ev_ebbc = on("dve", lambda h: h.tensor_copy(ebbc[:], pA[:, 0:NE]), waits=[ev], inc="dve")
    pfree[id(pA)] = rd(pA, ev_ebbc)

    pp_items = []
    if "prepass" not in DEV_SKIP:
        for i in range(NIC):
            pp_items.append(("all", d_wo, i * 128, NIC, s_wo, i))
        for i in range(NJ):
            pp_items.append(("all", d_sg, i * 128, NIC, s_sg, i))
        for i in range(NJ):
            pp_items.append(("all", d_su, i * 128, NIC, s_su, i))
        for e in range(NE):
            for j2 in range(2 * NJ):
                pp_items.append(("all", d_gu[e], j2 * 128, NIC, s_gu, e * 2 * NJ + j2))

    pp_free = [[] for _ in range(2)]
    ready_all = {}
    pp_cursor = [0]

    def emit_pp(n_items):
        i0 = pp_cursor[0]
        for i in range(i0, min(i0 + n_items, len(pp_items))):
            grp, src, col0, nch, dst, sidx = pp_items[i]
            slot = i % 2
            w = nch * 128
            stF = sb_olf[:, slot * 2048: slot * 2048 + w]
            stH = sb_oh[:, slot * 4096: slot * 4096 + w]
            stL = sb_oh[:, slot * 4096 + 2048: slot * 4096 + 2048 + w]
            e_in = on("sync", (lambda h, src=src, col0=col0, nch=nch, stF=stF:
                               h.dma_start(
                                   out=stF.rearrange("p (ic m) -> p ic m", ic=nch),
                                   in_=src[:, col0:col0 + 128].rearrange(
                                       "(ic p) m -> p ic m", p=128))),
                      waits=pp_free[slot], inc=f"sPin{slot}", amt=16)
            e_hi = on("act", (lambda h, stH=stH, stF=stF: h.copy(stH, stF)),
                      waits=[e_in], inc="act")
            e_lo = on("dve", (lambda h, stL=stL, stF=stF, stH=stH:
                              h.tensor_sub(stL, stF, stH)),
                      waits=[e_in, e_hi], inc="dve")
            e_oh = on("gp", (lambda h, dst=dst, sidx=sidx, nch=nch, stH=stH:
                             h.dma_start(
                                 out=dst[sidx, 0:nch * 128, :].rearrange(
                                     "(ic p) m -> p ic m", p=128),
                                 in_=stH.rearrange("p (ic m) -> p ic m", ic=nch))),
                     waits=[e_hi], inc=f"sPoh{slot}", amt=16)
            e_ol = on("gp", (lambda h, dst=dst, sidx=sidx, nch=nch, stL=stL:
                             h.dma_start(
                                 out=dst[sidx, nch * 128:2 * nch * 128, :].rearrange(
                                     "(ic p) m -> p ic m", p=128),
                                 in_=stL.rearrange("p (ic m) -> p ic m", ic=nch))),
                     waits=[e_lo], inc=f"sPol{slot}", amt=16)
            pp_free[slot] = [e_oh, e_ol]
            ready_all[f"h{slot}"] = e_oh
            ready_all[f"l{slot}"] = e_ol
        pp_cursor[0] = min(i0 + n_items, len(pp_items))

    last_S_user = {h_: ev_S0 for h_ in range(H)}
    S_up_ev = {h_: ev_S0 for h_ in range(H)}
    x_free_ev = None
    xn_free_ev = None
    sbout_free_ev = None
    orms_free_ev = None
    acc_free_ev = None
    acc_free_ev2 = None
    def ready_evs():
        return list(ready_all.values())
    h_free_ev = None
    sb_fa_free = None
    sb_fa_free_act = None

    def xnh_rhs(ic):
        return sb_xnh[:, ic * TT:(ic + 1) * TT]

    def xnl_rhs(ic):
        return sb_xnl[:, ic * TT:(ic + 1) * TT]

    def xn32_rhs(ic):
        return sb_acc[:, ic * TT:(ic + 1) * TT]

    def slab_load(ring, dst3, sidx, nch, ready):
        return ring.load(
            (lambda h, sl, d=dst3, s=sidx, nch=nch: h.dma_start(
                out=ringA_b[sl].rearrange("p (hl ic m) -> p hl ic m", hl=2, ic=nch),
                in_=d[s].rearrange("(hl ic p) m -> p hl ic m", hl=2, p=128))),
            extra_waits=ready)

    for t in range(N_TILE):
        own = t >= N_PRE
        if own and pp_cursor[0] < len(pp_items):
            emit_pp(len(pp_items))

        ln_last = None
        prev_var_reader = None
        prev_xn = None
        for tg in range(NG):
            r0 = t * TT + tg * 128
            x_ev = on("gp", dma(sb_x[:], d_x[r0:r0 + 128, :]),
                      waits=[x_free_ev], inc="sX", amt=16)
            xg = sb_x[:]
            e_sum = on("dve", lambda h, xg=xg: h.tensor_reduce(sb_m[:], xg, AX.X, ALU.add),
                       waits=[x_ev, prev_xn], inc="dve")
            e_sq = on("act", lambda h, xg=xg: h.activation(
                sb_h[:, 0:D], xg, AF.Square, accum_out=sb_sq[:]),
                waits=[x_ev, h_free_ev, prev_var_reader], inc="act")
            e_m = on("dve", lambda h: h.tensor_scalar(sb_m[:], sb_m[:], INV_D, None, ALU.mult),
                     waits=[e_sum], inc="dve")
            e_va = on("dve", lambda h: h.tensor_scalar(sb_var[:], sb_sq[:], INV_D, None, ALU.mult),
                      waits=[e_sq, e_m], inc="dve")
            e_mq = on("dve", lambda h: h.tensor_mul(sb_msq[:], sb_m[:], sb_m[:]),
                      waits=[e_m], inc="dve")
            e_vb = on("dve", lambda h: h.tensor_sub(sb_var[:], sb_var[:], sb_msq[:]),
                      waits=[e_va, e_mq], inc="dve")
            e_vc = on("dve", lambda h: h.tensor_scalar(sb_var[:], sb_var[:], EPS, None, ALU.add),
                      waits=[e_vb], inc="dve")
            e_sd = on("act", lambda h: h.activation(sb_var[:], sb_var[:], AF.Sqrt),
                      waits=[e_vc], inc="act")
            e_r = on("dve", lambda h: h.reciprocal(sb_rstd[:], sb_var[:]),
                     waits=[e_sd], inc="dve")
            prev_var_reader = e_r
            e_xn = on("dve", lambda h, xg=xg: h.tensor_scalar(
                sb_out[:], xg, sb_m[:, 0:1], sb_rstd[:, 0:1], ALU.subtract, ALU.mult),
                waits=[e_r, sbout_free_ev], inc="dve")
            ln_last = e_xn
            prev_xn = e_xn
            x_free_ev = e_xn
            tr_last = None
            for ic in range(NIC):
                pT = pG if ic % 2 == 0 else pH
                sl = (ic // 2) % 4
                csl = slice(ic * TT + tg * 128, ic * TT + (tg + 1) * 128)
                e_t = on("pe", lambda h, pT=pT, sl=sl, ic=ic: h.transpose(
                    pT[:, sl * 128:(sl + 1) * 128], sb_out[:, ic * 128:(ic + 1) * 128],
                    ident[:]),
                    waits=[e_xn, boot_ld, pfree[id(pT)], pread[id(pT)], xn_free_ev],
                    inc="pe")
                if "c32" not in DEV_SKIP:
                    e_c32 = on("dve", lambda h, pT=pT, sl=sl, csl=csl: h.tensor_copy(
                        sb_acc[:, csl], pT[:, sl * 128:(sl + 1) * 128]),
                        waits=[e_t,
                               acc_free_ev if (tg == 0 and ic == 0) else None,
                               acc_free_ev2 if (tg == 0 and ic == 0) else None],
                        inc="dve")
                    rd(pT, e_c32)
                    last_ev = e_c32
                else:
                    e_c32 = None
                    last_ev = e_t
                if own and "xnsplit" not in DEV_SKIP:
                    e_ch = on("act", lambda h, pT=pT, sl=sl, csl=csl: h.copy(
                        sb_xnh[:, csl], pT[:, sl * 128:(sl + 1) * 128]),
                        waits=[e_t, e_c32], inc="act")
                    rd(pT, e_ch)
                    e_cl = on("dve", lambda h, pT=pT, sl=sl, csl=csl: h.tensor_sub(
                        sb_xnl[:, csl], pT[:, sl * 128:(sl + 1) * 128], sb_xnh[:, csl]),
                        waits=[e_t, e_ch], inc="dve")
                    rd(pT, e_cl)
                    last_ev = e_cl
                pfree[id(pT)] = last_ev
                tr_last = e_t
            sbout_free_ev = tr_last
        xn_ready = S.now("dve")
        xn_ready_act = S.now("act")
        if t == N_PRE:
            tap("xnh", sb_xnh[:], xn_ready)
            tap("xnl", sb_xnl[:], xn_ready)
        xn_free_ev = None

        pq_guard = [None, None]
        last_oh = None
        last_ol = None
        for hh in (range(H) if "attn" in DEV_PHASES else []):
            if not own:
                emit_pp(2)
            if own:
                sq_slot, sq_ev = ringA.load(lambda h, sl, hh=hh: h.dma_start(
                    out=ringA_f[sl].rearrange("p (ic m) -> p ic m", ic=NIC),
                    in_=slab16(d_wq, hh * 128)))
                bufq = ringA_f[sq_slot]
            sk_slot, sk_ev = ringA.load(lambda h, sl, hh=hh: h.dma_start(
                out=ringA_f[sl].rearrange("p (ic m) -> p ic m", ic=NIC),
                in_=slab16(d_wk, hh * 128)))
            bufk = ringA_f[sk_slot]

            ev_qf = None
            ev_fb_q = None
            if own:
                ev_q = pe_acc(pA, [
                    (bufq[:, ic * 128:(ic + 1) * 128], xn32_rhs(ic),
                     [sq_ev, xn_ready] if ic == 0 else [])
                    for ic in range(NIC)], extra_waits=[pfree[id(pA)]])
                ringA.release(sq_slot, ev_q)
                ev_qe = on("act", lambda h: h.copy(sb_pq[:], pA[:]),
                           waits=[ev_q, pq_guard[0], pq_guard[1]], inc="act")
                pfree[id(pA)] = rd(pA, ev_qe)
                ev_fa_q = on("pe", lambda h: mm(h, pC[:], fmw["fq1_t"][:], sb_pq[:], True, True),
                             waits=[ev_qe, pfree[id(pC)], pread[id(pC)]], inc="pe")
                ev_fb_q = on("pe", lambda h: mm(h, pD[:], fmw["fq2_t"][:], sb_pq[:], True, True),
                             waits=[pfree[id(pD)], pread[id(pD)]], inc="pe")
                e1 = on("dve", lambda h: h.tensor_scalar(
                    sb_fa[:], pC[:], fmb["fq1_b"][:, 0:1], scale_col[:, 0:1],
                    ALU.add, ALU.mult),
                    waits=[ev_fa_q, sb_fa_free, sb_fa_free_act], inc="dve")
                pfree[id(pC)] = rd(pC, e1)
                e2 = on("dve", lambda h: h.tensor_scalar(
                    sb_fb2[:], pD[:], fmb["fq2_b"][:, 0:1], None, ALU.add),
                    waits=[ev_fb_q], inc="dve")
                pfree[id(pD)] = rd(pD, e2)
                ev_qf = on("dve", lambda h: h.tensor_mul(sb_qf[:], sb_fa[:], sb_fb2[:]),
                           waits=[e1, e2], inc="dve")
                if t == N_PRE and hh == H - 1:
                    tap("qf", sb_qf[:], ev_qf)

            ev_k = pe_acc(pB, [
                (bufk[:, ic * 128:(ic + 1) * 128], xn32_rhs(ic),
                 [sk_ev, xn_ready] if ic == 0 else [])
                for ic in range(NIC)], extra_waits=[pfree[id(pB)]])
            ringA.release(sk_slot, ev_k)
            ev_ke = on("act", lambda h: h.copy(sb_pq[:], pB[:]),
                       waits=[ev_k, ev_fb_q], inc="act")
            pfree[id(pB)] = rd(pB, ev_ke)
            ev_fa_k = on("pe", lambda h: mm(h, pC[:], fmw["fk1_t"][:], sb_pq[:], True, True),
                         waits=[ev_ke, pfree[id(pC)], pread[id(pC)]], inc="pe")
            ev_fb_k = on("pe", lambda h: mm(h, pD[:], fmw["fk2_t"][:], sb_pq[:], True, True),
                         waits=[pfree[id(pD)], pread[id(pD)]], inc="pe")
            e1 = on("dve", lambda h: h.tensor_scalar(
                sb_fa[:], pC[:], fmb["fk1_b"][:, 0:1], None, ALU.add),
                waits=[ev_fa_k, ev_qf, sb_fa_free, sb_fa_free_act], inc="dve")
            pfree[id(pC)] = rd(pC, e1)
            e2 = on("dve", lambda h: h.tensor_scalar(
                sb_fb2[:], pD[:], fmb["fk2_b"][:, 0:1], None, ALU.add),
                waits=[ev_fb_k, ev_qf], inc="dve")
            pfree[id(pD)] = rd(pD, e2)
            ev_kf = on("dve", lambda h: h.tensor_mul(sb_kf[:], sb_fa[:], sb_fb2[:]),
                       waits=[e1, e2], inc="dve")
            if not own:
                sb_fa_free = ev_kf
                sb_fa_free_act = None
            if t == N_PRE and hh == H - 1:
                tap("kf", sb_kf[:], ev_kf)

            sv_slot, sv_ev = ringA.load(lambda h, sl, hh=hh: h.dma_start(
                out=ringA_f[sl].rearrange("p (ic m) -> p ic m", ic=NIC),
                in_=slab16(d_wv, hh * 128)))
            bufv = ringA_f[sv_slot]
            ev_v = pe_acc(pA, [
                (bufv[:, ic * 128:(ic + 1) * 128], xn32_rhs(ic),
                 [sv_ev] if ic == 0 else [])
                for ic in range(NIC)], extra_waits=[pfree[id(pA)]])
            ringA.release(sv_slot, ev_v)
            ev_ve = on("act", lambda h: h.copy(sb_pv[:], pA[:]), waits=[ev_v, ev_kf], inc="act")
            pfree[id(pA)] = rd(pA, ev_ve)

            kt_evs, vt_evs = [], []
            for n in (range(NG) if "attn_trk" in DEV_PHASES else []):
                e_t = on("pe", lambda h, n=n: h.transpose(
                    pE[:, n * 128:(n + 1) * 128], sb_kf[:, n * 128:(n + 1) * 128], ident[:]),
                    waits=[ev_kf, pread[id(pE)]] + ([pfree[id(pE)]] if n == 0 else []),
                    inc="pe")
                e_c = on("dve", lambda h, n=n: h.tensor_copy(
                    sb_ktok[:, n * 128:(n + 1) * 128], pE[:, n * 128:(n + 1) * 128]),
                    waits=[e_t], inc="dve")
                rd(pE, e_c)
                kt_evs.append(e_c)
            if kt_evs:
                pfree[id(pE)] = kt_evs[-1]
            for n in (range(NG) if "attn_trv" in DEV_PHASES else []):
                e_t = on("pe", lambda h, n=n: h.transpose(
                    pF[:, n * 128:(n + 1) * 128], sb_pv[:, n * 128:(n + 1) * 128], ident[:]),
                    waits=[ev_ve, pread[id(pF)]] + ([pfree[id(pF)]] if n == 0 else []),
                    inc="pe")
                e_c = on("dve", lambda h, n=n: h.tensor_copy(
                    sb_vtok[:, n * 128:(n + 1) * 128], pF[:, n * 128:(n + 1) * 128]),
                    waits=[e_t], inc="dve")
                rd(pF, e_c)
                vt_evs.append(e_c)
            if vt_evs:
                pfree[id(pF)] = vt_evs[-1]

            last_oh = None
            last_ol = None
            for n in (range(NG) if "attn_scan" in DEV_PHASES else []):
                cs = slice(n * 128, (n + 1) * 128)
                if own:
                    e_sT = on("pe", lambda h, cs=cs: mm(
                        h, pG[:, cs], sb_kf[:, cs], sb_qf[:, cs], True, True),
                        waits=[ev_kf, ev_qf, pread[id(pG)]]
                              + ([pfree[id(pG)]] if n == 0 else []), inc="pe")
                    e_sm = on("dve", lambda h, cs=cs: h.tensor_mul(
                        sb_sm[:, cs], pG[:, cs], tril[:]), waits=[e_sT], inc="dve")
                    rd(pG, e_sm)
                    e_int = on("pe", lambda h, cs=cs, hh=hh: mm(
                        h, pH[:, cs], sb_S[:, hh * 128:(hh + 1) * 128], sb_qf[:, cs],
                        True, False),
                        waits=[S_up_ev[hh], ev_qf, pread[id(pH)]]
                              + ([pfree[id(pH)]] if n == 0 else []),
                        inc="pe")
                    last_S_user[hh] = e_int
                    e_o = on("pe", lambda h, cs=cs: mm(
                        h, pH[:, cs], sb_vtok[:, cs], sb_sm[:, cs], False, True),
                        waits=[e_sm, vt_evs[n]], inc="pe")
                e_kv = on("pe", lambda h, cs=cs: mm(
                    h, pE[:, cs], sb_ktok[:, cs], sb_vtok[:, cs], True, True),
                    waits=[kt_evs[n], vt_evs[n], kt_evs[-1], pread[id(pE)]], inc="pe")
                e_Sup = on("dve", lambda h, cs=cs, hh=hh: h.tensor_add(
                    sb_S[:, hh * 128:(hh + 1) * 128], sb_S[:, hh * 128:(hh + 1) * 128],
                    pE[:, cs]),
                    waits=[e_kv, last_S_user[hh]], inc="dve")
                rd(pE, e_Sup)
                S_up_ev[hh] = e_Sup
                if n == NG - 1:
                    pfree[id(pE)] = e_Sup

                if own:
                    e_oc = on("dve", lambda h, cs=cs: h.tensor_copy(sb_o[:, cs], pH[:, cs]),
                              waits=[e_o, ev_kf], inc="dve")
                    rd(pH, e_oc)
                    e_osq = on("act", lambda h, cs=cs: h.activation(
                        sb_osq[:, cs], pH[:, cs], AF.Square), waits=[e_o], inc="act")
                    rd(pH, e_osq)
                    e_ss = on("pe", lambda h, cs=cs: mm(
                        h, pG[0:1, cs], ones_col[:], sb_osq[:, cs], True, True),
                        waits=[e_osq, e_sm, pread[id(pG)]], inc="pe")
                    e_st = on("dve", lambda h, cs=cs: h.tensor_scalar(
                        sb_stat[:, cs], pG[0:1, cs], INV_DK, EPS, ALU.mult, ALU.add),
                        waits=[e_ss], inc="dve")
                    rd(pG, e_st)
                    e_sd = on("act", lambda h, cs=cs: h.activation(
                        sb_stat[:, cs], sb_stat[:, cs], AF.Sqrt), waits=[e_st], inc="act")
                    e_rc = on("dve", lambda h, cs=cs: h.reciprocal(
                        sb_stat[:, cs], sb_stat[:, cs]), waits=[e_sd], inc="dve")
                    e_bc = on("pe", lambda h, cs=cs: mm(
                        h, pF[:, cs], ones_row[:], sb_stat[0:1, cs], True, True),
                        waits=[e_rc, vt_evs[-1], pread[id(pF)]], inc="pe")
                    e_or = on("dve", lambda h, cs=cs: h.scalar_tensor_tensor(
                        sb_pq[:, cs], sb_o[:, cs], normw[:, 0:1], pF[:, cs],
                        ALU.mult, ALU.mult),
                        waits=[e_oc, e_bc], inc="dve")
                    rd(pF, e_or)
                    osl = slice(hh * TT + n * 128, hh * TT + (n + 1) * 128)
                    e_oh2 = on("act", lambda h, cs=cs, osl=osl: h.copy(
                        sb_oh[:, osl], sb_pq[:, cs]),
                        waits=[e_or,
                               orms_free_ev if (hh == 0 and n == 0) else None]
                              + (ready_evs() if (t == N_PRE and hh == 0 and n == 0) else []),
                        inc="act")
                    e_ol2 = on("dve", lambda h, cs=cs, osl=osl: h.tensor_sub(
                        sb_ol[:, osl], sb_pq[:, cs], sb_oh[:, osl]),
                        waits=[e_or, e_oh2],
                        inc="dve")
                    last_oh = e_oh2
                    last_ol = e_ol2
                    if n == NG - 1:
                        pfree[id(pF)] = e_or
                        pfree[id(pH)] = e_or
                        pfree[id(pG)] = e_st
                        sb_fa_free = e_or
                        sb_fa_free_act = None
            if not own:
                pfree[id(pE)] = S_up_ev[hh]
                pfree[id(pF)] = vt_evs[-1]
                sb_fa_free = ev_kf
                sb_fa_free_act = None
                pq_guard = [None, None]
            else:
                pq_guard = [last_ol, last_oh]

        last_orms_write = S.now("dve")
        last_orms_write_act = S.now("act")

        acc_guard = None
        if own:
            last_wo_mm = None
            for oc in (range(NIC) if ("wo" in DEV_PHASES and "attn" in DEV_PHASES) else []):
                so_slot, so_ev = slab_load(ringA, s_wo, oc, NIC, ready_evs())
                bufo = ringA_b[so_slot]
                pW = pA if oc % 2 == 0 else pB
                ev_w = pe_acc3(pW, bufo, H,
                               lambda hh2: sb_oh[:, hh2 * TT:(hh2 + 1) * TT],
                               lambda hh2: sb_ol[:, hh2 * TT:(hh2 + 1) * TT],
                               lambda hh2: ([so_ev, last_orms_write, last_orms_write_act]
                                            if hh2 == 0 else []),
                               extra_waits=[pfree[id(pW)]])
                ringA.release(so_slot, ev_w)
                last_wo_mm = ev_w
                ev_e = on("dve", lambda h, oc=oc, pW=pW: h.tensor_copy(
                    sb_acc[:, oc * TT:(oc + 1) * TT], pW[:]),
                    waits=[ev_w], inc="dve")
                pfree[id(pW)] = rd(pW, ev_e)
            orms_free_ev = last_wo_mm
            if t == N_PRE:
                acc_guard = tap("acc0", sb_acc[:, 0:TT], S.now("dve"))

            rec_bufs = [sb_pq, sb_qf]
            prev_rmm = [None, None]
            qf_pq_last = S.now("pe")
            ev_r = None
            for ic in range(NIC):
                rb = rec_bufs[ic % 2]
                e_rec = on("dve", lambda h, rb=rb, ic=ic: h.tensor_add(
                    rb[:], sb_xnh[:, ic * TT:(ic + 1) * TT],
                    sb_xnl[:, ic * TT:(ic + 1) * TT]),
                    waits=[prev_rmm[ic % 2],
                           qf_pq_last if ic < 2 else None,
                           last_ol if ic == 0 else None,
                           last_oh if ic == 0 else None], inc="dve")
                ev_r = on("pe", lambda h, rb=rb, ic=ic: mm(
                    h, pC[0:NE, :], routerw[:, ic * NE:(ic + 1) * NE], rb[:],
                    ic == 0, ic == NIC - 1),
                    waits=[e_rec] + ([pfree[id(pC)], pread[id(pC)]] if ic == 0 else []),
                    inc="pe")
                prev_rmm[ic % 2] = ev_r
            ev_ze = on("act", lambda h: h.copy(sb_z[:], pC[0:NE, :]), waits=[ev_r], inc="act")
            pfree[id(pC)] = rd(pC, ev_ze)
            last_sg = None
            comb_t_last = None
            prev_cb = None
            for g in range(NG):
                gs = slice(g * 128, (g + 1) * 128)
                e_t = on("pe", lambda h, gs=gs, g=g: mm(
                    h, pE[:, g * 128: g * 128 + NE], sb_z[:, gs], ident[0:NE, 0:NE],
                    True, True),
                    waits=[ev_ze, pread[id(pE)]] + ([pfree[id(pE)]] if g == 0 else []),
                    inc="pe")
                e_sg = on("act", lambda h, g=g: h.activation(
                    sb_sc[:, g * NE:(g + 1) * NE], pE[:, g * 128: g * 128 + NE], AF.Sigmoid),
                    waits=[e_t], inc="act")
                rd(pE, e_sg)
                last_sg = e_sg
                e_b = on("dve", lambda h, g=g: h.tensor_add(
                    sb_bia[:], sb_sc[:, g * NE:(g + 1) * NE], ebbc[:]),
                    waits=[e_sg, ev_ebbc, prev_cb], inc="dve")
                e_m8 = on("dve", lambda h: h.max(out=sb_top[:], in_=sb_bia[:]),
                          waits=[e_b], inc="dve")
                e_mk = on("dve", lambda h: h.tensor_scalar(
                    sb_msk[:], sb_bia[:], sb_top[:, 3:4], None, ALU.is_ge),
                    waits=[e_m8], inc="dve")
                e_w4 = on("dve", lambda h, g=g: h.tensor_mul(
                    sb_w4[:], sb_msk[:], sb_sc[:, g * NE:(g + 1) * NE]),
                    waits=[e_mk], inc="dve")
                e_s4 = on("dve", lambda h: h.tensor_reduce(
                    sb_s4[:], sb_w4[:], AX.X, ALU.add),
                    waits=[e_w4], inc="dve")
                e_s4b = on("dve", lambda h: h.tensor_scalar(
                    sb_s4[:], sb_s4[:], 1e-20, None, ALU.add), waits=[e_s4], inc="dve")
                e_rc = on("dve", lambda h: h.reciprocal(sb_s4[:], sb_s4[:]),
                          waits=[e_s4b], inc="dve")
                e_cb = on("dve", lambda h: h.tensor_scalar(
                    sb_w4[:], sb_w4[:], sb_s4[:, 0:1], None, ALU.mult),
                    waits=[e_rc], inc="dve")
                prev_cb = e_cb
                e_ct = on("pe", lambda h, g=g: mm(
                    h, pC[0:NE, g * 128:(g + 1) * 128], sb_w4[:], ident[:], True, True),
                    waits=[e_cb, pread[id(pC)]] + ([pfree[id(pC)]] if g == 0 else []),
                    inc="pe")
                comb_t_last = e_ct
            pfree[id(pE)] = last_sg
            ev_comb = on("dve", lambda h: h.tensor_copy(sb_comb[:], pC[0:NE, :]),
                         waits=[comb_t_last], inc="dve")
            if t == N_PRE:
                tap("comb", sb_comb[:], ev_comb)
            pfree[id(pC)] = rd(pC, ev_comb)

            cbc_free = ev_comb
            last_acc_add = None
            last_h0 = None
            for e in (range(NE + 1) if "experts" in DEV_PHASES else []):
                routed = e < NE
                ev_cbc = None
                if routed:
                    e_rx = on("pe", lambda h, e=e: mm(
                        h, pD[0:1, :], ident[0:NE, e:e + 1], sb_comb[:], True, True),
                        waits=[ev_comb, pfree[id(pD)], pread[id(pD)], cbc_free], inc="pe")
                    e_rs = on("dve", lambda h: h.tensor_copy(sb_stat[0:1, :], pD[0:1, :]),
                              waits=[e_rx], inc="dve")
                    rd(pD, e_rs)
                    ev_cbc = on("pe", lambda h: mm(
                        h, pD[:], ones_row[:], sb_stat[0:1, :], True, True),
                        waits=[e_rs, pread[id(pD)]], inc="pe")
                h_evs = []
                for j in range(NJ):
                    if routed:
                        gslot, g_ev = slab_load(ringA, s_gu, e * 2 * NJ + 2 * j, NIC, ready_evs())
                        uslot, u_ev = slab_load(ringA, s_gu, e * 2 * NJ + 2 * j + 1, NIC, ready_evs())
                    else:
                        gslot, g_ev = slab_load(ringA, s_sg, j, NIC, ready_evs())
                        uslot, u_ev = slab_load(ringA, s_su, j, NIC, ready_evs())
                    bufg, bufu = ringA_b[gslot], ringA_b[uslot]
                    ev_g = pe_acc3(pA, bufg, NIC, xnh_rhs, xnl_rhs,
                                   lambda ic: [g_ev] if ic == 0 else [],
                                   extra_waits=[pfree[id(pA)]])
                    ringA.release(gslot, ev_g)
                    ev_u = pe_acc3(pB, bufu, NIC, xnh_rhs, xnl_rhs,
                                   lambda ic: [u_ev] if ic == 0 else [],
                                   extra_waits=[pfree[id(pB)]])
                    ringA.release(uslot, ev_u)
                    e_sil = on("act", lambda h: h.activation(sb_g[:], pA[:], AF.Silu),
                               waits=[ev_g, last_h0], inc="act")
                    pfree[id(pA)] = rd(pA, e_sil)
                    e_ue = on("act", lambda h: h.copy(sb_u[:], pB[:]),
                              waits=[ev_u, last_h0], inc="act")
                    pfree[id(pB)] = rd(pB, e_ue)
                    e_h0 = on("dve", lambda h, j=j: h.tensor_mul(
                        sb_h[:, j * TT:(j + 1) * TT], sb_g[:], sb_u[:]),
                        waits=[e_sil, e_ue], inc="dve")
                    last_h0 = e_h0
                    if routed:
                        e_h = on("dve", lambda h, j=j: h.tensor_mul(
                            sb_h[:, j * TT:(j + 1) * TT],
                            sb_h[:, j * TT:(j + 1) * TT], pD[:]),
                            waits=[e_h0, ev_cbc], inc="dve")
                        rd(pD, e_h)
                    else:
                        e_h = e_h0
                    h_evs.append(e_h)
                if routed:
                    cbc_free = h_evs[-1]
                    pfree[id(pD)] = h_evs[-1]
                for oc in range(NIC):
                    if routed:
                        dslot, d_ev = ringB.load(lambda h, sl, e=e, oc=oc: h.dma_start(
                            out=ringB_bufs[sl].rearrange("p (ic m) -> p ic m", ic=NJ),
                            in_=slab16(d_dw[e], oc * 128)))
                    else:
                        dslot, d_ev = ringB.load(lambda h, sl, oc=oc: h.dma_start(
                            out=ringB_bufs[sl].rearrange("p (ic m) -> p ic m", ic=NJ),
                            in_=slab16(d_sd, oc * 128)))
                    bufd = ringB_bufs[dslot]
                    pW = pE if oc % 2 == 0 else pF
                    ev_d = pe_acc(pW, [
                        (bufd[:, jc * 128:(jc + 1) * 128], sb_h[:, jc * TT:(jc + 1) * TT],
                         [d_ev, h_evs[-1]] if jc == 0 else [])
                        for jc in range(NJ)], extra_waits=[pfree[id(pW)]])
                    ringB.release(dslot, ev_d)
                    ev_a = on("dve", lambda h, oc=oc, pW=pW: h.tensor_add(
                        sb_acc[:, oc * TT:(oc + 1) * TT],
                        sb_acc[:, oc * TT:(oc + 1) * TT], pW[:]),
                        waits=[ev_d, acc_guard], inc="dve")
                    pfree[id(pW)] = rd(pW, ev_a)
                    last_acc_add = ev_a
                    h_free_ev = ev_d

            last_acc_read = None
            for tg in range(NG):
                r0x = t * TT + tg * 128
                x_ev = on("gp", dma(sb_x[:], d_x[r0x:r0x + 128, :]),
                          waits=[x_free_ev], inc="sX", amt=16)
                st_evs = []
                for oc in range(NIC):
                    pT = pG if oc % 2 == 0 else pH
                    sl = (oc // 2) % 4
                    e_t = on("pe", lambda h, pT=pT, sl=sl, oc=oc, tg=tg: h.transpose(
                        pT[:, sl * 128:(sl + 1) * 128],
                        sb_acc[:, oc * TT + tg * 128: oc * TT + (tg + 1) * 128], ident[:]),
                        waits=[pfree[id(pT)], pread[id(pT)],
                               last_acc_add if (tg == 0 and oc == 0) else None],
                        inc="pe")
                    last_acc_read = e_t
                    e_a = on("dve", lambda h, pT=pT, sl=sl, oc=oc: h.tensor_add(
                        sb_out[:, oc * 128:(oc + 1) * 128],
                        pT[:, sl * 128:(sl + 1) * 128],
                        sb_x[:, oc * 128:(oc + 1) * 128]),
                        waits=[e_t, x_ev, sbout_free_ev if oc == 0 else None], inc="dve")
                    pfree[id(pT)] = rd(pT, e_a)
                    st_evs.append(e_a)
                x_free_ev = st_evs[-1]
                r0 = (t - N_PRE) * TT + tg * 128
                ev_st = on("gp", dma(d_out[r0:r0 + 128, :], sb_out[:]),
                           waits=[st_evs[-1]], inc="sO", amt=16)
                sbout_free_ev = ev_st
            acc_free_ev = last_acc_read
            acc_free_ev2 = None
            xn_free_ev = S.now("pe")
        else:
            xn_free_ev = S.now("pe")
            acc_free_ev = S.now("pe")
            acc_free_ev2 = None

    total_sO = S.counts.get("sO", 0)
    on("gp", lambda h: h.nop(), waits=[("sO", total_sO)])

    sem_names = (["pe", "dve", "act", "gp", "sBoot", "sA0", "sA1",
                  "sB0", "sB1", "sX", "sO", "sTap"]
                 + [f"sPin{i}" for i in range(4)]
                 + [f"sPoh{i}" for i in range(4)]
                 + [f"sPol{i}" for i in range(4)])
    import contextlib
    with contextlib.ExitStack() as stack:
        semmap = {n: stack.enter_context(nc.semaphore("sem_" + n)) for n in sem_names}
        block = stack.enter_context(nc.Block())

        def make_runner(name):
            def run(h):
                waited = {}
                for waits, emit, inc in S.items[name]:
                    neww = []
                    for (sem, val) in waits:
                        if val > waited.get(sem, 0):
                            neww.append((sem, val))
                            waited[sem] = val
                    for (sem, val) in neww[:-1]:
                        h.wait_ge(semmap[sem], val)
                    inst = emit(h)
                    if neww:
                        sem, val = neww[-1]
                        try:
                            inst._wait_ge(semmap[sem], val)
                        except Exception:
                            raise
                    if inc is not None:
                        inst.then_inc(semmap[inc[0]], inc[1])
            return run

        block.tensor(make_runner("pe"))
        block.vector(make_runner("dve"))
        block.scalar(make_runner("act"))
        block.sync(make_runner("sync"))
        block.gpsimd(make_runner("gp"))

    return nc


def _prep_inputs(inputs):
    f = lambda a: np.ascontiguousarray(np.asarray(a, dtype=np.float32))
    x = f(inputs["L0_attn_x"])
    gu = f(inputs["L0_ffn_gate_up_proj"])
    dw = f(inputs["L0_ffn_down_proj"])
    com = {
        "wq_t": f(np.asarray(inputs["L0_attn_wq"]).T),
        "wk_t": f(np.asarray(inputs["L0_attn_wk"]).T),
        "wv_t": f(np.asarray(inputs["L0_attn_wv"]).T),
        "wo_t": f(np.asarray(inputs["L0_attn_wo"]).T),
        "fq1_t": f(np.asarray(inputs["L0_attn_fq1_w"]).T),
        "fq2_t": f(np.asarray(inputs["L0_attn_fq2_w"]).T),
        "fk1_t": f(np.asarray(inputs["L0_attn_fk1_w"]).T),
        "fk2_t": f(np.asarray(inputs["L0_attn_fk2_w"]).T),
        "fq1_b": f(inputs["L0_attn_fq1_b"]).reshape(128, 1),
        "fq2_b": f(inputs["L0_attn_fq2_b"]).reshape(128, 1),
        "fk1_b": f(inputs["L0_attn_fk1_b"]).reshape(128, 1),
        "fk2_b": f(inputs["L0_attn_fk2_b"]).reshape(128, 1),
        "norm_w": f(inputs["L0_attn_norm_w"]).reshape(128, 1),
        "router_wt": f(np.asarray(inputs["L0_ffn_router_w"]).T),
        "e_bias": f(inputs["L0_ffn_e_bias"]).reshape(1, NE),
        "sg_t": f(np.asarray(inputs["L0_ffn_shared_gate_w"]).T),
        "su_t": f(np.asarray(inputs["L0_ffn_shared_up_w"]).T),
        "sd_t": f(np.asarray(inputs["L0_ffn_shared_down_w"]).T),
    }
    gut = np.transpose(gu, (0, 2, 1))
    idx = []
    for j in range(NJ):
        idx.extend(range(j * 128, (j + 1) * 128))
        idx.extend(range(768 + j * 128, 768 + (j + 1) * 128))
    com["gu_t"] = f(gut[:, :, idx])
    com["dw_t"] = f(np.transpose(dw, (0, 2, 1)))

    in_maps = []
    for c in range(8):
        b, q = c // 4, c % 4
        own = x[b, q * 2048:(q + 1) * 2048]
        pre = x[b, :q * 2048]
        pad = np.zeros((N_PRE * TT - q * 2048, D), np.float32)
        xc = np.concatenate([pad, pre, own], axis=0)
        m = dict(com)
        m["x"] = np.ascontiguousarray(xc)
        in_maps.append(m)
    return in_maps


def kernel(**inputs):
    in_maps = _prep_inputs(inputs)
    nc = build_nc()
    res = run_bass_kernel_spmd(nc, in_maps, core_ids=list(range(8)))
    out = np.empty((2, 8192, 2048), np.float32)
    for c in range(8):
        b, q = c // 4, c % 4
        out[b, q * 2048:(q + 1) * 2048] = res.results[c]["out"]
    return out
